# version 11
# baseline (speedup 1.0000x reference)
import numpy as np
from contextlib import ExitStack
import concourse.bass as bass
import concourse.mybir as mybir
from concourse.bass_utils import run_bass_kernel_spmd

F32 = mybir.dt.float32
BF16 = mybir.dt.bfloat16
AF = mybir.ActivationFunctionType
OP = mybir.AluOpType
AX = mybir.AxisListType.X

D = 1024
FF = 4096
E = 8
NH = 16
HD = 64
KP = D // 128
MG = FF // 128
MD = KP
KD = FF // 128
NT = 512
TB = NT // 128
EPS_LN = 1e-5
EPS_DEN = 1e-6
BIG = 1.0e30
N_CORES = 8
T_FULL = 8192
DBG_TAKE = 30
PHASES = "all"


class Sch:
    def __init__(self, sems):
        self.sems = sems
        self.cnt = {k: 0 for k in sems}
        self.q = {k: [] for k in "tvsgy"}
        self.lastw = {qn: {} for qn in "tvsgy"}

    def wait(self, qn, ev):
        if ev is None:
            return
        if isinstance(ev, list):
            for e in ev:
                self.wait(qn, e)
            return
        s, v = ev
        if self.lastw[qn].get(s, -1) >= v:
            return
        self.lastw[qn][s] = v
        h = self.sems[s]
        self.q[qn].append(lambda e, h=h, v=v: e.wait_ge(h, v))

    def op(self, qn, fn, inc=None, waits=(), d=1):
        for ev in waits:
            self.wait(qn, ev)
        if inc is None:
            self.q[qn].append(fn)
            return None
        self.cnt[inc] += d
        ev = (inc, self.cnt[inc])
        h = self.sems[inc]
        self.q[qn].append(lambda e, fn=fn, h=h, d=d: fn(e).then_inc(h, d))
        return ev

    def dma(self, qn, out, in_, inc, waits=()):
        if inc == "WL":
            self._wl = (getattr(self, "_wl", 0) + 1) % 4
            inc = ("WL", "WL2", "WL3", "WL4")[self._wl]
        ch = getattr(self, "_chain", None)
        if ch is None:
            ch = self._chain = {}
        waits = list(waits) + [ch.get((qn, inc))]
        ev = self.op(qn, lambda e, o=out, i=in_: e.dma_start(out=o, in_=i),
                     inc=inc, waits=waits, d=16)
        ch[(qn, inc)] = ev
        return ev

    def edge(self, qn, name):
        self.wait(qn, (name, self.cnt[name]))

    def barrier(self):
        return [(n, self.cnt[n]) for n in ("T", "V", "S", "WL", "AL", "ST", "G")
                if self.cnt[n] > 0]


def build_program(n_tiles=16):
    T = NT * n_tiles
    nc = bass.Bass("TRN2", target_bir_lowering=False)
    dtens = lambda name, shape, dty, kind: nc.dram_tensor(name, shape, dty, kind=kind).ap()

    xT = dtens("xT", [D, T], F32, "ExternalInput")
    wgF = dtens("wgF", [MG, 128, E, KP, 128], F32, "ExternalInput")
    wuF = dtens("wuF", [MG, 128, E, KP, 128], F32, "ExternalInput")
    wdF = dtens("wdF", [MD, 128, E, KD, 128], F32, "ExternalInput")
    wqF = dtens("wqF", [D, D], F32, "ExternalInput")
    wkF = dtens("wkF", [D, D], F32, "ExternalInput")
    wvF = dtens("wvF", [D, D], F32, "ExternalInput")
    rguF = dtens("rguF", [D, 16], F32, "ExternalInput")
    rdF = dtens("rdF", [FF, 8], F32, "ExternalInput")
    browF = dtens("browF", [1, 3 * D], F32, "ExternalInput")
    gbF = dtens("gbF", [128, 16], F32, "ExternalInput")
    idF = dtens("idF", [128, 128], F32, "ExternalInput")
    outT = dtens("outT", [D, T], F32, "ExternalOutput")

    wgB = dtens("wgB", [MG, 128, E, KP, 128], BF16, "Internal")
    wuB = dtens("wuB", [MG, 128, E, KP, 128], BF16, "Internal")
    wdB = dtens("wdB", [MD, 128, E, KD, 128], BF16, "Internal")
    wqB = dtens("wqB", [D, D], BF16, "Internal")
    wkB = dtens("wkB", [D, D], BF16, "Internal")
    wvB = dtens("wvB", [D, D], BF16, "Internal")
    xbT = dtens("xbT", [D, T], BF16, "Internal")
    x1T = dtens("x1T", [D, T], F32, "Internal")
    xn1T = dtens("xn1T", [D, T], BF16, "Internal")
    x2T = dtens("x2T", [D, T], F32, "Internal")
    xn2T = dtens("xn2T", [D, T], BF16, "Internal")

    def featmaj(ap_):
        return ap_.rearrange("(k p) n -> p k n", p=128)

    with ExitStack() as ctx:
        en = ctx.enter_context
        x_res = en(nc.sbuf_tensor("x_res", [128, KP * NT], F32))
        xn_b = en(nc.sbuf_tensor("xn_b", [128, KP * NT], BF16))
        xs = en(nc.sbuf_tensor("xs", [128, E * KP * NT], BF16))
        t1 = en(nc.sbuf_tensor("t1", [128, MG * NT], BF16))
        acc = en(nc.sbuf_tensor("acc", [128, KP * NT], F32))
        wW0 = en(nc.sbuf_tensor("wW0", [128, E * KP * 128], BF16))
        wW1 = en(nc.sbuf_tensor("wW1", [128, E * KP * 128], BF16))
        wD0 = en(nc.sbuf_tensor("wD0", [128, KD * 128], BF16))
        wD1 = en(nc.sbuf_tensor("wD1", [128, KD * 128], BF16))
        bcd = en(nc.sbuf_tensor("bcd", [128, 2 * NT], BF16))
        bcg = en(nc.sbuf_tensor("bcg", [128, 2 * NT], BF16))
        sigt = en(nc.sbuf_tensor("sigt", [128, 2 * NT], BF16))
        ln0 = en(nc.sbuf_tensor("ln0", [128, NT], F32))
        ln1 = en(nc.sbuf_tensor("ln1", [128, NT], F32))
        ln2 = en(nc.sbuf_tensor("ln2", [128, NT], F32))
        tmpd = en(nc.sbuf_tensor("tmpd", [128, NT], F32))
        ln3 = en(nc.sbuf_tensor("ln3", [128, NT], F32))
        kv_sb = wW1[0:64, 0:NH * HD]
        ksf = en(nc.sbuf_tensor("ksf", [128, 8], F32))
        ksb = en(nc.sbuf_tensor("ksb", [128, 8], BF16))
        KS = en(nc.sbuf_tensor("KS", [128, KP * 16], BF16))
        rtok = en(nc.sbuf_tensor("rtok", [128, TB * 16], F32))
        rtok_bf = en(nc.sbuf_tensor("rtok_bf", [128, TB * 16], BF16))
        rgu_sb = en(nc.sbuf_tensor("rgu_sb", [128, KP * 16], BF16))
        rd_sb = en(nc.sbuf_tensor("rd_sb", [128, KD * 8], BF16))
        gb_sb = en(nc.sbuf_tensor("gb_sb", [128, 16], F32))
        idb = en(nc.sbuf_tensor("idb", [128, 128], BF16))
        idf = en(nc.sbuf_tensor("idf", [128, 128], F32))
        rt_sb = ln0[0:16, :]
        ones128 = en(nc.sbuf_tensor("ones128", [128, 128], BF16))
        ones1b = en(nc.sbuf_tensor("ones1b", [1, 128], BF16))
        onescol = en(nc.sbuf_tensor("onescol", [128, 1], BF16))
        eps_t = en(nc.sbuf_tensor("eps_t", [128, 1], F32))
        comb = en(nc.sbuf_tensor("comb", [128, TB * 16], F32))
        comb_bf = en(nc.sbuf_tensor("comb_bf", [128, TB * 16], BF16))
        combd = en(nc.sbuf_tensor("combd", [128, TB * 8], F32))
        combd_bf = en(nc.sbuf_tensor("combd_bf", [128, TB * 8], BF16))
        rm1 = en(nc.sbuf_tensor("rm1", [128, 12], F32))
        rm2 = en(nc.sbuf_tensor("rm2", [128, 12], F32))
        rp1 = en(nc.sbuf_tensor("rp1", [128, 12], F32))
        rp2 = en(nc.sbuf_tensor("rp2", [128, 12], F32))
        rmsk = en(nc.sbuf_tensor("rmsk", [128, TB * 24], F32))
        rl2 = en(nc.sbuf_tensor("rl2", [128, TB * 24], F32))
        crow = en(nc.sbuf_tensor("crow", [1, 2 * NT], BF16))
        brow_sb = xs[0:1, 25600:25600 + 3 * D]

        pA = en(nc.psum_tensor("pA", [128, 512], F32))
        pB = en(nc.psum_tensor("pB", [128, 512], F32))
        pC = en(nc.psum_tensor("pC", [128, 512], F32))
        pD = en(nc.psum_tensor("pD", [128, 512], F32))
        pE = en(nc.psum_tensor("pE", [128, 512], F32))
        pF = en(nc.psum_tensor("pF", [128, 1024], BF16))
        pG = en(nc.psum_tensor("pG", [128, 512], F32))
        pH = en(nc.psum_tensor("pH", [128, 512], F32))

        sems = {n: en(nc.semaphore(n)) for n in ("T", "V", "S", "G", "WL", "WL2", "WL3", "WL4", "AL", "AL2", "ST")}
        sch = Sch(sems)

        rel = {b: None for b in ("A", "B", "C", "D", "E", "F", "G", "H")}
        bank = {"A": pA, "B": pB, "C": pC, "D": pD, "E": pE, "F": pF, "G": pG, "H": pH}

        ev_ones = sch.op("g", lambda e: e.memset(ones128[:], 1.0), inc="G")
        ev_ones1b = sch.op("g", lambda e: e.memset(ones1b[:], 1.0), inc="G")
        ev_onescol = sch.op("g", lambda e: e.memset(onescol[:], 1.0), inc="G")
        ev_ksz = sch.op("g", lambda e: e.memset(KS[:], 0.0), inc="G")
        ev_eps = sch.op("g", lambda e: e.memset(eps_t[:], EPS_LN), inc="G")
        ev_gb = sch.dma("g", gb_sb[:], gbF, "AL")
        ev_idb = sch.dma("g", idb[:], idF, "AL")
        ev_idf = sch.dma("g", idf[:], idF, "AL")
        ev_rgu = sch.dma("g", rgu_sb[:].rearrange("p (k c) -> p k c", c=16),
                         rguF.rearrange("(k p) c -> p k c", p=128), "AL")
        ev_rd = sch.dma("g", rd_sb[:].rearrange("p (k c) -> p k c", c=8),
                        rdF.rearrange("(k p) c -> p k c", p=128), "AL")
        ev_wg = [sch.dma("g", wgB[c:c + 4], wgF[c:c + 4], "AL") for c in range(0, MG, 4)]
        ev_wu = [sch.dma("g", wuB[c:c + 4], wuF[c:c + 4], "AL") for c in range(0, MG, 4)]
        ev_wd = [sch.dma("g", wdB[c:c + 1], wdF[c:c + 1], "AL") for c in range(MD)]
        ev_xbt = sch.dma("g", xbT, xT, "AL")
        ev_wq = sch.dma("g", wqB, wqF, "AL")
        ev_wk = sch.dma("g", wkB, wkF, "AL")
        ev_wv = sch.dma("g", wvB, wvF, "AL")

        def emit_ln(src_f32, src_bf, dst_bf, wait_src, b1n, b2n, sq_of,
                    cast_bf=False, apply=True, wait_dst_free=None, wait_sq_free=None):
            b1, b2 = bank[b1n], bank[b2n]
            if cast_bf:
                ev_cast = None
                for k in range(KP):
                    ev_cast = sch.op("s", lambda e, k=k: e.activation(
                        src_bf(k), src_f32(k), AF.Copy), inc="S",
                        waits=([wait_src, wait_sq_free] if k == 0 else ()))
            else:
                ev_cast = wait_src
            ev_sq = None
            for k in range(KP):
                ev_sq = sch.op("s", lambda e, k=k: e.activation(
                    sq_of(k), src_f32(k), AF.Square), inc="S",
                    waits=([wait_src, wait_sq_free] if k == 0 else ()))
            ev_s1 = None
            for k in range(KP):
                ev_s1 = sch.op("t", lambda e, k=k, b1=b1: e.matmul(
                    b1[:], ones128[:], src_bf(k), start=(k == 0), stop=(k == KP - 1)),
                    inc="T" if k == KP - 1 else None,
                    waits=([ev_cast if cast_bf else wait_src, ev_ones, rel[b1n]]
                           if k == 0 else ()))
            ev_s2 = None
            for k in range(KP):
                ev_s2 = sch.op("t", lambda e, k=k, b2=b2: e.matmul(
                    b2[:], ones128[:], sq_of(k), start=(k == 0), stop=(k == KP - 1)),
                    inc="T" if k == KP - 1 else None,
                    waits=([ev_sq, rel[b2n]] if k == 0 else ()))
            c1 = 1.0 / D
            sch.op("v", lambda e: e.tensor_scalar_mul(ln0[:], b1[:], c1),
                   inc="V", waits=[ev_s1])
            ev_bfree = sch.op("v", lambda e: e.tensor_scalar_mul(ln1[:], b2[:], c1),
                              inc="V", waits=[ev_s2])
            rel[b1n] = ev_bfree
            rel[b2n] = ev_bfree
            sch.edge("v", "V")
            sch.op("v", lambda e: e.tensor_mul(ln2[:], ln0[:], ln0[:]), inc="V")
            sch.edge("v", "V")
            ev = sch.op("v", lambda e: e.tensor_sub(ln1[:], ln1[:], ln2[:]), inc="V")
            ev = sch.op("s", lambda e: e.activation(ln2[:], ln1[:], AF.Sqrt,
                                                    bias=eps_t[:, 0:1]),
                        inc="S", waits=[ev, ev_eps])
            ev = sch.op("v", lambda e: e.reciprocal(ln3[:], ln2[:]), inc="V", waits=[ev])
            sch.edge("v", "V")
            ev_ab = sch.op("v", lambda e: e.scalar_tensor_tensor(
                ln1[:], ln0[:], -1.0, ln3[:], OP.mult, OP.mult), inc="V")
            sch.edge("v", "V")
            if not apply:
                return ev_ab, ev_bfree
            for k in range(KP):
                sch.op("v", lambda e, k=k: e.tensor_mul(dst_bf(k), src_f32(k), ln3[:]),
                       inc="V" if k == KP - 1 else None,
                       waits=([wait_dst_free] if (k == 0 and wait_dst_free) else ()))
            sch.edge("v", "V")
            ev_ap = None
            for k in range(KP):
                ev_ap = sch.op("v", lambda e, k=k: e.tensor_add(dst_bf(k), dst_bf(k), ln1[:]),
                               inc="V" if k == KP - 1 else None)
            sch.edge("v", "V")
            return ev_ap, ev_bfree

        def emit_top2(psrc, n_r, combo, combo_bf, moff, mskoff, wait_logits):
            for j in range(n_r):
                sch.op("v", lambda e, j=j: e.reduce_max(
                    rm1[:, moff + j:moff + j + 1], psrc(j), axis=AX),
                    inc="V" if j == n_r - 1 else None,
                    waits=[wait_logits] if j == 0 else ())
            sch.edge("v", "V")
            for j in range(n_r):
                sch.op("v", lambda e, j=j: e.tensor_scalar(
                    rmsk[:, mskoff + j * 8: mskoff + (j + 1) * 8], psrc(j),
                    rm1[:, moff + j:moff + j + 1], None, OP.is_equal),
                    inc="V" if j == n_r - 1 else None)
            sch.edge("v", "V")
            for j in range(n_r):
                sch.op("v", lambda e, j=j: e.scalar_tensor_tensor(
                    rl2[:, mskoff + j * 8: mskoff + (j + 1) * 8],
                    rmsk[:, mskoff + j * 8: mskoff + (j + 1) * 8], -BIG,
                    psrc(j), OP.mult, OP.add),
                    inc="V" if j == n_r - 1 else None)
            sch.edge("v", "V")
            for j in range(n_r):
                sch.op("v", lambda e, j=j: e.reduce_max(
                    rm2[:, moff + j:moff + j + 1],
                    rl2[:, mskoff + j * 8: mskoff + (j + 1) * 8], axis=AX),
                    inc="V" if j == n_r - 1 else None)
            sch.edge("v", "V")
            ev = sch.op("v", lambda e: e.tensor_sub(
                rp1[:, moff:moff + n_r], rm1[:, moff:moff + n_r],
                rm2[:, moff:moff + n_r]), inc="V")
            ev = sch.op("s", lambda e: e.activation(
                rp1[:, moff:moff + n_r], rp1[:, moff:moff + n_r], AF.Sigmoid),
                inc="S", waits=[ev])
            ev = sch.op("v", lambda e: e.tensor_scalar(
                rp2[:, moff:moff + n_r], rp1[:, moff:moff + n_r], -1.0, 1.0,
                OP.mult, OP.add), inc="V", waits=[ev])
            sch.edge("v", "V")
            for j in range(n_r):
                sch.op("v", lambda e, j=j: e.tensor_scalar(
                    combo[:, j * 8:(j + 1) * 8],
                    rmsk[:, mskoff + j * 8: mskoff + (j + 1) * 8],
                    rp1[:, moff + j:moff + j + 1], None, OP.mult),
                    inc="V" if j == n_r - 1 else None)
            sch.edge("v", "V")
            for j in range(n_r):
                sch.op("v", lambda e, j=j: e.tensor_scalar(
                    rmsk[:, mskoff + j * 8: mskoff + (j + 1) * 8],
                    rl2[:, mskoff + j * 8: mskoff + (j + 1) * 8],
                    rm2[:, moff + j:moff + j + 1], None, OP.is_equal),
                    inc="V" if j == n_r - 1 else None)
            sch.edge("v", "V")
            for j in range(n_r):
                sch.op("v", lambda e, j=j: e.scalar_tensor_tensor(
                    combo[:, j * 8:(j + 1) * 8],
                    rmsk[:, mskoff + j * 8: mskoff + (j + 1) * 8],
                    rp2[:, moff + j:moff + j + 1],
                    combo[:, j * 8:(j + 1) * 8], OP.mult, OP.add),
                    inc="V" if j == n_r - 1 else None)
            sch.edge("v", "V")
            ev_bf = sch.op("v", lambda e: e.tensor_copy(combo_bf[:], combo[:]), inc="V")
            sch.edge("v", "V")
            return ev_bf

        pF_rel = [None]
        crow_rel = [None, None]
        bcg_rel = [None, None]
        bcd_rel = [None, None]

        def emit_bcast(col_of, slot, wait_src, dst_bf=None, dst_rel=None,
                       extra_tr_waits=()):
            half = slot
            ev_tr = None
            for tb in range(TB):
                ev_tr = sch.op("t", lambda e, tb=tb, half=half: e.transpose(
                    pF[0:1, half * 512 + tb * 128: half * 512 + tb * 128 + 128],
                    col_of(tb), idb[:]),
                    inc="T" if tb == TB - 1 else None,
                    waits=([wait_src, ev_idb, pF_rel[0]] + list(extra_tr_waits))
                    if tb == 0 else ())
            ev_cr = sch.op("v", lambda e, half=half: e.tensor_copy(
                crow[0:1, half * NT:(half + 1) * NT],
                pF[0:1, half * 512:(half + 1) * 512]), inc="V",
                waits=[ev_tr, crow_rel[half]])
            ev_bc = sch.op("t", lambda e: e.matmul(
                pH[:], ones1b[:],
                crow[0:1, half * NT:(half + 1) * NT],
                start=True, stop=True),
                inc="T", waits=[ev_cr, ev_ones1b, rel["H"]])
            crow_rel[half] = ev_bc
            pF_rel[0] = ev_cr
            ev = sch.op("s", lambda e, d=dst_bf: e.activation(
                d, pH[:], AF.Copy), inc="S", waits=[ev_bc, dst_rel])
            rel["H"] = ev
            return ev


        def bcast_steps(col_of, slot, wait_src_getter, dst_bf, dst_rel_getter,
                        done_cb):
            st = {}
            def s1():
                half = slot
                ev_tr = None
                for tb in range(TB):
                    ev_tr = sch.op("t", lambda e, tb=tb, half=half: e.transpose(
                        pF[0:1, half * 512 + tb * 128: half * 512 + tb * 128 + 128],
                        col_of(tb), idb[:]),
                        inc="T" if tb == TB - 1 else None,
                        waits=([wait_src_getter(), ev_idb, pF_rel[0]]) if tb == 0 else ())
                st["ev_tr"] = ev_tr
            def s2():
                half = slot
                ev_cr = sch.op("v", lambda e, half=half: e.tensor_copy(
                    crow[0:1, half * NT:(half + 1) * NT],
                    pF[0:1, half * 512:(half + 1) * 512]), inc="V",
                    waits=[st["ev_tr"], crow_rel[half]])
                pF_rel[0] = ev_cr
                st["ev_cr"] = ev_cr
            def s3():
                half = slot
                ev_bc = sch.op("t", lambda e, half=half: e.matmul(
                    pH[:], ones1b[:],
                    crow[0:1, half * NT:(half + 1) * NT],
                    start=True, stop=True),
                    inc="T", waits=[st["ev_cr"], ev_ones1b, rel["H"]])
                crow_rel[half] = ev_bc
                ev = sch.op("v", lambda e, d=dst_bf: e.tensor_copy(
                    d, pH[:]), inc="V", waits=[ev_bc, dst_rel_getter()])
                rel["H"] = ev
                done_cb(ev)
            return [s1, s2, s3]

        st_events = {}

        free = {"x_res": None, "xn": None, "xs": None, "acc": None}
        sig_rel = [None, None]
        pAB_rel = rel
        CYC4 = ("A", "B", "G")
        wW_rel = [None, None]
        wD_rel = [None, None]

        def emit_ffn(inv, srcT, src_bfT, dstT, dst_name):
            def make_head(ti):
                ctx = {}
                c0 = ti * NT

                def preA():
                    w_x = [free["x_res"]]
                    w_xn = [free["xn"]]
                    if inv == 2:
                        w_x.append(st_events.get(("x2", ti)))
                        w_xn.append(st_events.get(("xn2", ti)))
                    else:
                        w_xn.append(ev_xbt)
                    sch.dma(
                        "s", x_res[:].rearrange("p (k n) -> p k n", n=NT),
                        featmaj(srcT)[:, :, c0:c0 + NT], "AL2", waits=w_x)
                    ev_pair = sch.dma(
                        "s", xn_b[:].rearrange("p (k n) -> p k n", n=NT),
                        featmaj(src_bfT)[:, :, c0:c0 + NT], "AL2", waits=w_xn)
                    ctx["ev_x"] = ev_pair
                    ctx["ev_xn"] = ev_pair

                def preLN():
                    xr = lambda k: x_res[:, k * NT:(k + 1) * NT]
                    xnk = lambda k: xn_b[:, k * NT:(k + 1) * NT]
                    sqk = lambda k: xs[:, k * NT:(k + 1) * NT]
                    if inv == 1:
                        ev_ln, _ = emit_ln(xr, xnk, xnk, [ctx["ev_xn"], ctx["ev_x"]],
                                           "C", "D", sqk, wait_sq_free=free["xs"])
                        ctx["ev_sq_last"] = ("S", sch.cnt["S"])
                        ctx["ev_apply1"] = ("V", sch.cnt["V"])
                        ctx["ev_xn_ready"] = ev_ln
                    else:
                        ctx["ev_sq_last"] = None
                        ctx["ev_apply1"] = None
                        ctx["ev_xn_ready"] = ctx["ev_xn"]

                def preB():
                    ev_rt = None
                    for k in range(KP):
                        ev_rt = sch.op("t", lambda e, k=k: e.matmul(
                            pH[0:16, 0:NT], rgu_sb[:, k * 16:(k + 1) * 16],
                            xn_b[:, k * NT:(k + 1) * NT],
                            start=(k == 0), stop=(k == KP - 1),
                            skip_group_check=True),
                            inc="T" if k == KP - 1 else None,
                            waits=([ctx["ev_xn_ready"], ev_rgu, rel["H"]]
                                   if k == 0 else ()))
                    ev_rte = sch.op("v", lambda e: e.tensor_copy(
                        rt_sb[:], pH[0:16, 0:NT]), inc="V", waits=[ev_rt])
                    rel["H"] = ev_rte
                    ev_rtt = None
                    for tb in range(TB):
                        ev_rtt = sch.op("t", lambda e, tb=tb: e.transpose(
                            pE[:, tb * 16:(tb + 1) * 16],
                            rt_sb[0:16, tb * 128:(tb + 1) * 128], idf[0:16, 0:16]),
                            inc="T" if tb == TB - 1 else None,
                            waits=[ev_rte, ev_idf, rel["E"]] if tb == 0 else ())
                    ctx["ev_comb"] = emit_top2(
                        lambda j: pE[:, (j // 2) * 16 + (j % 2) * 8:
                                     (j // 2) * 16 + (j % 2) * 8 + 8],
                        TB * 2, comb, comb_bf, 0, 0, ev_rtt)

                def xs_steps(ei):
                    slot = ei % 2
                    hstate = {}
                    steps = bcast_steps(
                        lambda tb, ei=ei: comb_bf[:, tb * 16 + ei:tb * 16 + ei + 1],
                        slot, lambda: ctx["ev_comb"],
                        bcg[:, slot * NT:(slot + 1) * NT],
                        lambda slot=slot: bcg_rel[slot],
                        lambda ev: hstate.update(ev_cp=ev))
                    def tts():
                        last = None
                        for k in range(KP):
                            last = sch.op("v", lambda e, ei=ei, k=k, slot=slot: e.tensor_tensor(
                                xs[:, (ei * KP + k) * NT:(ei * KP + k + 1) * NT],
                                xn_b[:, k * NT:(k + 1) * NT],
                                bcg[:, slot * NT:(slot + 1) * NT], OP.mult),
                                inc="V" if k == KP - 1 else None,
                                waits=[hstate["ev_cp"], ctx["ev_xn_ready"], free["xs"]]
                                if k == 0 else ())
                        bcg_rel[slot] = last
                        ctx["ev_xs_gate"] = last
                    return steps + [tts]
                hsteps = [preA, preLN, preB]
                for ei in range(E):
                    hsteps.extend(xs_steps(ei))
                return ctx, hsteps

            ctx0, thunks0 = make_head(0)
            for th in thunks0:
                th()
            heads = {0: ctx0}

            for ti in range(n_tiles):
                c0 = ti * NT
                ctx = heads.pop(ti)
                ev_init = None
                for k in range(KP):
                    ev_init = sch.op("s", lambda e, k=k: e.activation(
                        acc[:, k * NT:(k + 1) * NT], x_res[:, k * NT:(k + 1) * NT],
                        AF.Copy, scale=2.0), inc="S" if k == KP - 1 else None,
                        waits=[free["acc"], ctx["ev_x"]] if k == 0 else ())
                free["x_res"] = [ev for ev in (ev_init, ctx["ev_sq_last"],
                                               ctx["ev_apply1"]) if ev]

                ev_gate = [None] * MG
                for m in range(MG):
                    buf = wW0 if m % 2 == 0 else wW1
                    bn = CYC4[m % 3]
                    bk = bank[bn]
                    ev_ld = sch.dma("y", buf[:].rearrange("p (e k c) -> p e k c", k=KP, c=128),
                                    wgB[m], "WL",
                                    waits=[ev_wg[m // 4], wW_rel[m % 2]])
                    w0 = [ev_ld, pAB_rel[bn]] + ([ctx["ev_xs_gate"]] if m == 0 else [])
                    last = None
                    for ei in range(E):
                        for k in range(KP):
                            last = sch.op("t", lambda e, ei=ei, k=k, buf=buf, bk=bk: e.matmul(
                                bk[:], buf[:, (ei * KP + k) * 128:(ei * KP + k + 1) * 128],
                                xs[:, (ei * KP + k) * NT:(ei * KP + k + 1) * NT],
                                start=(ei == 0 and k == 0),
                                stop=(ei == E - 1 and k == KP - 1)),
                                inc="T" if (ei == E - 1 and k == KP - 1) else None,
                                waits=w0 if (ei == 0 and k == 0) else ())
                    ev_gate[m] = last
                    wW_rel[m % 2] = last
                    ev_sg = sch.op("s", lambda e, m=m, bk=bk: e.activation(
                        sigt[:, (m % 2) * NT:(m % 2 + 1) * NT], bk[:], AF.Sigmoid),
                        inc="S", waits=[last, sig_rel[m % 2]])
                    ev_t1 = sch.op("v", lambda e, m=m, bk=bk: e.tensor_tensor(
                        t1[:, m * NT:(m + 1) * NT],
                        sigt[:, (m % 2) * NT:(m % 2 + 1) * NT], bk[:], OP.mult),
                        inc="V", waits=[ev_sg])
                    sig_rel[m % 2] = ev_t1
                    pAB_rel[bn] = ev_t1

                ev_xs_up = None
                for ei in range(E):
                    slot = ei % 2
                    ev_cp = emit_bcast(
                        lambda tb, ei=ei: comb_bf[:, tb * 16 + 8 + ei:tb * 16 + 8 + ei + 1],
                        slot, ctx["ev_comb"], dst_bf=bcg[:, slot * NT:(slot + 1) * NT],
                        dst_rel=bcg_rel[slot])
                    for k in range(KP):
                        ev_xs_up = sch.op("v", lambda e, ei=ei, k=k, slot=slot: e.tensor_tensor(
                            xs[:, (ei * KP + k) * NT:(ei * KP + k + 1) * NT],
                            xn_b[:, k * NT:(k + 1) * NT],
                            bcg[:, slot * NT:(slot + 1) * NT], OP.mult),
                            inc="V" if k == KP - 1 else None,
                            waits=[ev_cp, ev_gate[MG - 1]] if k == 0 else ())
                    bcg_rel[slot] = ev_xs_up
                free["xn"] = ev_xs_up

                ev_up = [None] * MG
                ev_drt = None
                for m in range(MG):
                    buf = wW0 if m % 2 == 0 else wW1
                    bn = CYC4[m % 3]
                    bk = bank[bn]
                    ev_ld = sch.dma("y", buf[:].rearrange("p (e k c) -> p e k c", k=KP, c=128),
                                    wuB[m], "WL", waits=[ev_wu[m // 4], wW_rel[m % 2]])
                    w0 = [ev_ld, pAB_rel[bn]] + ([ev_xs_up] if m == 0 else [])
                    last = None
                    for ei in range(E):
                        for k in range(KP):
                            last = sch.op("t", lambda e, ei=ei, k=k, buf=buf, bk=bk: e.matmul(
                                bk[:], buf[:, (ei * KP + k) * 128:(ei * KP + k + 1) * 128],
                                xs[:, (ei * KP + k) * NT:(ei * KP + k + 1) * NT],
                                start=(ei == 0 and k == 0),
                                stop=(ei == E - 1 and k == KP - 1)),
                                inc="T" if (ei == E - 1 and k == KP - 1) else None,
                                waits=w0 if (ei == 0 and k == 0) else ())
                    ev_up[m] = last
                    wW_rel[m % 2] = last
                    ev_int = sch.op("v", lambda e, m=m, bk=bk: e.tensor_tensor(
                        t1[:, m * NT:(m + 1) * NT],
                        t1[:, m * NT:(m + 1) * NT], bk[:], OP.mult),
                        inc="V", waits=[last])
                    pAB_rel[bn] = ev_int
                    for tb in range(TB):
                        ev_drt = sch.op("t", lambda e, m=m, tb=tb: e.matmul(
                            pE[:, 64 + tb * 8: 64 + tb * 8 + 8],
                            t1[:, m * NT + tb * 128: m * NT + tb * 128 + 128],
                            rd_sb[:, m * 8:(m + 1) * 8],
                            start=False, stop=(m == MG - 1),
                            skip_group_check=True),
                            inc="T" if (m == MG - 1 and tb == TB - 1) else None,
                            waits=([ev_int, ev_rd] +
                                   ([ctx["ev_comb"]] if m == 0 else [])) if tb == 0 else ())
                free["xs"] = ev_up[MG - 1]

                ev_comb_d = emit_top2(
                    lambda j: pE[:, 64 + j * 8: 64 + j * 8 + 8],
                    TB, combd, combd_bf, 8, TB * 16, ev_drt)
                rel["E"] = ev_comb_d

                if ti + 1 < n_tiles:
                    nctx, nthunks = make_head(ti + 1)
                    heads[ti + 1] = nctx
                    nthunks[0]()
                    nthunks = nthunks[1:]
                else:
                    nthunks = []
                micro = []
                sch._hd_taken = 0

                ev_res = None
                bcd_ev = {}
                def bcd_steps_for(ei):
                    slot = ei % 2
                    return bcast_steps(
                        lambda tb, ei=ei: combd_bf[:, tb * 8 + ei:tb * 8 + ei + 1],
                        slot, lambda: ev_comb_d,
                        bcd[:, slot * NT:(slot + 1) * NT],
                        lambda slot=slot: bcd_rel[slot],
                        lambda ev, ei=ei: bcd_ev.__setitem__(ei, ev))
                for stp in bcd_steps_for(0):
                    stp()
                for ei in range(E):
                    slot = ei % 2
                    if ei + 1 < E:
                        micro.extend(bcd_steps_for(ei + 1))
                    cap = DBG_TAKE if isinstance(DBG_TAKE, int) else (99 if DBG_TAKE else 0)
                    already = getattr(sch, "_hd_taken", 0)
                    take = min(5, len(nthunks), max(0, cap - already))
                    sch._hd_taken = already + take
                    micro.extend(nthunks[:take])
                    nthunks = nthunks[take:]
                    ev_bcd = bcd_ev[ei]
                    for m2 in range(MD):
                        bslot = (ei * MD + m2) % 2
                        buf = wD0 if bslot == 0 else wD1
                        bn = CYC4[(ei * MD + m2) % 3]
                        bk = bank[bn]
                        ev_ld = sch.dma("y", buf[:].rearrange("p (k c) -> p k c", c=128),
                                        wdB[m2, :, ei], "WL",
                                        waits=[ev_wd[m2], wD_rel[bslot]])
                        w0 = [ev_ld, pAB_rel[bn], ev_bcd]
                        if ei == 0 and m2 == 0:
                            w0.append(ev_init)
                        last = None
                        for k in range(KD):
                            last = sch.op("t", lambda e, k=k, buf=buf, bk=bk: e.matmul(
                                bk[:], buf[:, k * 128:(k + 1) * 128],
                                t1[:, k * NT:(k + 1) * NT],
                                start=(k == 0), stop=(k == KD - 1)),
                                inc="T" if k == KD - 1 else None,
                                waits=w0 if k == 0 else ())
                        wD_rel[bslot] = last
                        ev_t = sch.op("v", lambda e, bk=bk, slot=slot: e.tensor_tensor(
                            tmpd[:], bk[:], bcd[:, slot * NT:(slot + 1) * NT],
                            OP.mult), inc="V", waits=[last])
                        sch.edge("v", "V")
                        ev_ev = sch.op("v", lambda e, m2=m2: e.tensor_add(
                            acc[:, m2 * NT:(m2 + 1) * NT],
                            acc[:, m2 * NT:(m2 + 1) * NT], tmpd[:]), inc="V")
                        pAB_rel[bn] = ev_ev
                        bcd_rel[slot] = ev_t
                        if ei == E - 1:
                            sch.edge("v", "V")
                            ev_res = sch.op("v", lambda e, m2=m2: e.tensor_scalar_mul(
                                acc[:, m2 * NT:(m2 + 1) * NT],
                                acc[:, m2 * NT:(m2 + 1) * NT], 0.5), inc="V")
                        if micro:
                            micro.pop(0)()
                for th in micro:
                    th()
                for th in nthunks:
                    th()
                ev_st = sch.dma("s", featmaj(dstT)[:, :, c0:c0 + NT],
                                acc[:].rearrange("p (k n) -> p k n", n=NT), "ST",
                                waits=[ev_res])
                st_events[(dst_name, ti)] = ev_st
                free["acc"] = ev_st

        def emit_attn():
            bar = sch.barrier()
            ev_lwq = sch.dma("g", t1[:, 0:KP * 1024].rearrange("p (k c) -> p k c", c=1024),
                             featmaj(wqB), "AL", waits=[ev_wq] + bar)
            ev_lwk = sch.dma("g", t1[:, KP * 1024:2 * KP * 1024].rearrange(
                "p (k c) -> p k c", c=1024), featmaj(wkB), "AL", waits=[ev_wk])
            ev_lwv = sch.dma("g", wW0[:].rearrange("p (k c) -> p k c", c=1024),
                             featmaj(wvB), "AL", waits=[ev_wv])
            ev_brow = sch.dma("g", brow_sb, browF, "AL")
            kq = lambda k: t1[:, k * 1024:(k + 1) * 1024]
            kk = lambda k: t1[:, (KP + k) * 1024:(KP + k + 1) * 1024]
            kvw = lambda k: wW0[:, k * 1024:(k + 1) * 1024]
            ktok = lambda tb: xs[:, tb * 1024:(tb + 1) * 1024]
            vtok = lambda tb: xs[:, (TB + tb) * 1024:(TB + tb + 1) * 1024]
            sq_at = lambda k: xs[:, 8 * 1024 + k * NT: 8 * 1024 + (k + 1) * NT]

            def xnp(ti):
                return xn_b if ti % 2 == 0 else xs[:, 12288:16384]

            p1_free_buf = {0: free["xn"], 1: free["xs"]}
            p1_free_x = [free["x_res"]]
            p1_sq_free = [free["xs"]]
            p1ctx = {}

            def p1_head(ti):
                c0 = ti * NT
                buf = xnp(ti)
                ev_x = sch.dma("g", x_res[:].rearrange("p (k n) -> p k n", n=NT),
                               featmaj(x1T)[:, :, c0:c0 + NT], "AL",
                               waits=[p1_free_x[0], st_events[("x1", ti)]])
                ev_xr = sch.dma("g", buf.rearrange("p (k n) -> p k n", n=NT),
                                featmaj(x1T)[:, :, c0:c0 + NT], "AL",
                                waits=[p1_free_buf[ti % 2], st_events[("x1", ti)]])
                xr = lambda k: x_res[:, k * NT:(k + 1) * NT]
                xnk = lambda k, buf=buf: buf[:, k * NT:(k + 1) * NT]
                ev_ln, _ = emit_ln(xr, xnk, xnk, [ev_xr, ev_x], "G", "H", sq_at,
                                   wait_sq_free=p1_sq_free[0])
                p1_sq_free[0] = ("T", sch.cnt["T"])
                p1_free_x[0] = [ev_ln, ("S", sch.cnt["S"])]
                ev_stxn = sch.dma("g", featmaj(xn1T)[:, :, c0:c0 + NT],
                                  buf.rearrange("p (k n) -> p k n", n=NT), "ST",
                                  waits=[ev_ln])
                st_events[("xn1", ti)] = ev_stxn
                if PHASES == "p1":
                    sch.dma("g", featmaj(outT)[:, :, c0:c0 + NT],
                            buf.rearrange("p (k n) -> p k n", n=NT), "ST",
                            waits=[ev_ln])
                p1ctx[ti] = ev_ln

            def p1_proj(ti, which):
                buf = xnp(ti)
                ev_ln = p1ctx[ti]
                wt, boff, ev_lw = (kk, D, ev_lwk) if which == "k" else (kvw, 2 * D, ev_lwv)
                out = {}
                for tb in range(TB):
                    for hf in range(2):
                        bn = "A" if (tb * 2 + hf) % 2 == 0 else "B"
                        bk = bank[bn]
                        w0 = [ev_ln, ev_lw, pAB_rel[bn], ev_brow, ev_ones1b]
                        for k in range(KP):
                            sch.op("t", lambda e, tb=tb, hf=hf, k=k, bk=bk, wt=wt, buf=buf: e.matmul(
                                bk[:],
                                buf[:, k * NT + tb * 128: k * NT + tb * 128 + 128],
                                wt(k)[:, hf * 512:(hf + 1) * 512],
                                start=(k == 0), stop=False),
                                waits=w0 if k == 0 else ())
                        last = sch.op("t", lambda e, hf=hf, bk=bk, boff=boff: e.matmul(
                            bk[:], ones1b[:],
                            xs[0:1, 25600 + boff + hf * 512: 25600 + boff + (hf + 1) * 512],
                            start=False, stop=True), inc="T")
                        dst = (ktok if which == "k" else vtok)(tb)
                        if which == "k":
                            ev_e = sch.op("v", lambda e, dst=dst, hf=hf, bk=bk: e.tensor_relu(
                                dst[:, hf * 512:(hf + 1) * 512], bk[:]),
                                inc="V", waits=[last])
                        else:
                            ev_e = sch.op("v", lambda e, dst=dst, hf=hf, bk=bk: e.tensor_copy(
                                dst[:, hf * 512:(hf + 1) * 512], bk[:]),
                                inc="V", waits=[last])
                        pAB_rel[bn] = ev_e
                        out[tb] = ev_e
                return out

            p1_head(0)
            for ti in range(n_tiles):
                evk_k = p1_proj(ti, "k")
                if ti + 1 < n_tiles:
                    p1_head(ti + 1)
                evk_v = p1_proj(ti, "v")
                p1_free_buf[ti % 2] = [("T", sch.cnt["T"]), st_events[("xn1", ti)]]
                for tb in range(TB):
                    w_kv = [evk_k[tb], evk_v[tb]]
                    if ti == 0 and tb == 0:
                        w_kv += [rel["C"], rel["D"]]
                    for h in range(NH):
                        bn = "C" if h < 8 else "D"
                        bk = bank[bn]
                        hh = h % 8
                        st = (ti == 0 and tb == 0 and hh == 0)
                        ev_kv = sch.op("t", lambda e, tb=tb, h=h, hh=hh, bk=bk, st=st: e.matmul(
                            bk[0:64, hh * 64:(hh + 1) * 64],
                            ktok(tb)[:, h * HD:(h + 1) * HD],
                            vtok(tb)[:, h * HD:(h + 1) * HD],
                            start=st, stop=(ti == n_tiles - 1 and tb == TB - 1),
                            skip_group_check=True),
                            inc="T" if (ti == n_tiles - 1 and tb == TB - 1 and h == NH - 1) else None,
                            waits=w_kv if h == 0 else ())
                    for mm in range(KP):
                        st = (ti == 0 and tb == 0 and mm == 0)
                        ev_ks = sch.op("t", lambda e, tb=tb, mm=mm, st=st: e.matmul(
                            pE[:, mm:mm + 1],
                            ktok(tb)[:, mm * 128:(mm + 1) * 128], onescol[:],
                            start=st, stop=(ti == n_tiles - 1 and tb == TB - 1),
                            skip_group_check=True),
                            inc="T" if (ti == n_tiles - 1 and tb == TB - 1 and mm == KP - 1) else None,
                            waits=([ev_onescol, rel["E"]] if st else ()))
            free["xn"] = p1_free_buf[0] if (n_tiles - 1) % 2 == 1 else p1_free_buf[1]
            free["x_res"] = p1_free_x[0]

            ev_kv_done = ("T", sch.cnt["T"])
            ev1 = sch.op("v", lambda e: e.tensor_copy(wW1[0:64, 0:512], pC[0:64, :]),
                         inc="V", waits=[ev_kv_done])
            ev2 = sch.op("v", lambda e: e.tensor_copy(wW1[0:64, 512:1024], pD[0:64, :]),
                         inc="V")
            ev3 = sch.op("v", lambda e: e.tensor_copy(ksf[:], pE[:, 0:8]), inc="V")
            sch.edge("v", "V")
            ev4 = sch.op("v", lambda e: e.tensor_copy(ksb[:], ksf[:]), inc="V")
            sch.edge("v", "V")
            evK = None
            for k in range(KP):
                evK = sch.op("v", lambda e, k=k: e.tensor_copy(
                    KS[0:64, k * 16 + 2 * k: k * 16 + 2 * k + 1], ksb[0:64, k:k + 1]),
                    inc="V" if k == KP - 1 else None,
                    waits=[ev_ksz] if k == 0 else ())
            for k in range(KP):
                evK = sch.op("v", lambda e, k=k: e.tensor_copy(
                    KS[64:128, k * 16 + 2 * k + 1: k * 16 + 2 * k + 2],
                    ksb[64:128, k:k + 1]),
                    inc="V" if k == KP - 1 else None)
            sch.edge("v", "V")
            rel["C"] = ev1
            rel["D"] = ev2
            rel["E"] = ev3
            if PHASES == "p1":
                return

            qtok = lambda tb: xs[:, tb * 1024:(tb + 1) * 1024]
            qTs = lambda p: xs[:, 4096 + p * NT: 4096 + (p + 1) * NT]
            sq2 = lambda k: xs[:, 8192 + k * NT: 8192 + (k + 1) * NT]
            ybf = lambda k: xs[:, 12288 + k * NT: 12288 + (k + 1) * NT]
            qsh = lambda h: xs[0:64, 16384 + h * NT: 16384 + (h + 1) * NT]
            rbc = lambda sl: xs[0:64, 16384 + (16 + sl) * NT: 16384 + (17 + sl) * NT]
            free_xs2 = ("T", sch.cnt["T"])
            prev_xn2_st = None
            prev_x2_st = None

            for ti in range(n_tiles):
                c0 = ti * NT
                sch.dma("g", xn_b[:].rearrange("p (k n) -> p k n", n=NT),
                                featmaj(xn1T)[:, :, c0:c0 + NT], "AL",
                                waits=[free["xn"], st_events[("xn1", ti)]])
                ev_x1 = sch.dma("g", x_res[:].rearrange("p (k n) -> p k n", n=NT),
                                featmaj(x1T)[:, :, c0:c0 + NT], "AL",
                                waits=[free["x_res"]])
                ev_xn1 = ev_x1
                ev_q = {}
                for tb in range(TB):
                    for hf in range(2):
                        bn = "A" if (tb * 2 + hf) % 2 == 0 else "B"
                        bk = bank[bn]
                        w0 = [ev_xn1, ev_lwq, pAB_rel[bn], ev_brow]
                        for k in range(KP):
                            sch.op("t", lambda e, tb=tb, hf=hf, k=k, bk=bk: e.matmul(
                                bk[:],
                                xn_b[:, k * NT + tb * 128: k * NT + tb * 128 + 128],
                                kq(k)[:, hf * 512:(hf + 1) * 512],
                                start=(k == 0), stop=False),
                                waits=w0 if k == 0 else ())
                        last = sch.op("t", lambda e, hf=hf, bk=bk: e.matmul(
                            bk[:], ones1b[:], xs[0:1, 25600 + hf * 512: 25600 + (hf + 1) * 512],
                            start=False, stop=True), inc="T")
                        ev_q[(tb, hf)] = sch.op("v", lambda e, tb=tb, hf=hf, bk=bk: e.tensor_relu(
                            qtok(tb)[:, hf * 512:(hf + 1) * 512], bk[:]),
                            inc="V",
                            waits=[last, free_xs2] if (tb == 0 and hf == 0) else [last])
                        pAB_rel[bn] = ev_q[(tb, hf)]
                ev_qT = None
                for tb in range(TB):
                    half = tb % 2
                    ev_tr = None
                    for p4 in range(2):
                        for pi in range(4):
                            p = p4 * 4 + pi
                            ev_tr = sch.op("t", lambda e, tb=tb, p=p, pi=pi, half=half: e.transpose(
                                pF[:, half * 512 + pi * 128: half * 512 + (pi + 1) * 128],
                                qtok(tb)[:, p * 128:(p + 1) * 128], idb[:]),
                                inc="T" if pi == 3 else None,
                                waits=[ev_q[(tb, p // 4)], ev_idb, pF_rel[0]]
                                if pi == 0 else ())
                        for pi in range(4):
                            p = p4 * 4 + pi
                            sch.op("v", lambda e, tb=tb, p=p, pi=pi, half=half: e.tensor_copy(
                                qTs(p)[:, tb * 128:(tb + 1) * 128],
                                pF[:, half * 512 + pi * 128: half * 512 + (pi + 1) * 128]),
                                inc=None, waits=[ev_tr] if pi == 0 else ())
                            sch.op("v", lambda e, tb=tb, p=p, pi=pi, half=half: e.tensor_copy(
                                qsh(2 * p)[:, tb * 128:(tb + 1) * 128],
                                pF[0:64, half * 512 + pi * 128: half * 512 + (pi + 1) * 128]))
                            ev_cp = sch.op("v", lambda e, tb=tb, p=p, pi=pi, half=half: e.tensor_copy(
                                qsh(2 * p + 1)[:, tb * 128:(tb + 1) * 128],
                                pF[64:128, half * 512 + pi * 128: half * 512 + (pi + 1) * 128]),
                                inc="V")
                        pF_rel[0] = ev_cp
                        ev_qT = ev_cp
                ev_den = None
                for tb in range(TB):
                    for k in range(KP):
                        ev_den = sch.op("t", lambda e, tb=tb, k=k: e.matmul(
                            pE[:, tb * 16: tb * 16 + 16],
                            qTs(k)[:, tb * 128:(tb + 1) * 128],
                            KS[:, k * 16:(k + 1) * 16],
                            start=(tb == 0 and k == 0), stop=(k == KP - 1),
                            skip_group_check=True),
                            inc="T" if (tb == TB - 1 and k == KP - 1) else None,
                            waits=[ev_qT, evK, rel["E"]] if (tb == 0 and k == 0) else ())
                ev_r = sch.op("v", lambda e: e.tensor_scalar_max(
                    rtok[:], pE[:, 0:TB * 16], EPS_DEN), inc="V", waits=[ev_den])
                sch.edge("v", "V")
                ev_r = sch.op("v", lambda e: e.reciprocal(rtok[:], rtok[:]), inc="V")
                sch.edge("v", "V")
                ev_rbf = sch.op("v", lambda e: e.tensor_copy(rtok_bf[:], rtok[:]), inc="V")
                sch.edge("v", "V")
                rel["E"] = ev_rbf
                ev_y = []
                htr = {}
                hbc = {}
                rbc_rel = [None, None]

                def stageA(h):
                    half = h % 2
                    ev_tr = None
                    for tb in range(TB):
                        ev_tr = sch.op("t", lambda e, tb=tb, half=half, h=h: e.transpose(
                            pF[0:1, half * 512 + tb * 128: half * 512 + tb * 128 + 128],
                            rtok_bf[:, tb * 16 + h: tb * 16 + h + 1], idb[:]),
                            inc="T" if tb == TB - 1 else None,
                            waits=[ev_rbf, pF_rel[0]] if tb == 0 else ())
                    htr[h] = ev_tr

                def stageB(h):
                    half = h % 2
                    ev_cr = sch.op("v", lambda e, half=half: e.tensor_copy(
                        crow[0:1, half * NT:(half + 1) * NT],
                        pF[0:1, half * 512:(half + 1) * 512]), inc="V",
                        waits=[htr[h], crow_rel[half]])
                    pF_rel[0] = ev_cr
                    ev_bc = sch.op("t", lambda e, half=half: e.matmul(
                        pH[0:64, :], ones1b[:, 0:64],
                        crow[0:1, half * NT:(half + 1) * NT], start=True, stop=True),
                        inc="T", waits=[ev_cr, rel["H"]])
                    crow_rel[half] = ev_bc
                    ev_rc = sch.op("v", lambda e, half=half: e.tensor_copy(
                        rbc(half), pH[0:64, :]), inc="V",
                        waits=[ev_bc, rbc_rel[half]])
                    rel["H"] = ev_rc
                    hbc[h] = ev_rc

                def stageC(h):
                    half = h % 2
                    bnc = "C" if half == 0 else "D"
                    bkc = bank[bnc]
                    ev_ctx = sch.op("t", lambda e, h=h, bkc=bkc: e.matmul(
                        bkc[0:64, :], wW1[0:64, h * HD:(h + 1) * HD], qsh(h),
                        start=True, stop=True),
                        inc="T", waits=[ev1, ev2, ev_qT, rel[bnc]])
                    p = h // 2
                    r0 = (h % 2) * 64
                    ev_m = sch.op("v", lambda e, p=p, r0=r0, bkc=bkc, half=half: e.tensor_tensor(
                        acc[r0:r0 + 64, p * NT:(p + 1) * NT], bkc[0:64, :],
                        rbc(half), OP.mult),
                        inc="V",
                        waits=[ev_ctx, hbc[h], free["acc"]] if h == 0 else [ev_ctx, hbc[h]])
                    rbc_rel[half] = ev_m
                    sch.edge("v", "V")
                    ev_a = sch.op("v", lambda e, p=p, r0=r0: e.tensor_add(
                        acc[r0:r0 + 64, p * NT:(p + 1) * NT],
                        acc[r0:r0 + 64, p * NT:(p + 1) * NT],
                        xn_b[r0:r0 + 64, p * NT:(p + 1) * NT]),
                        inc="V", waits=[ev_xn1])
                    ev_y.append(ev_a)
                    rel[bnc] = ev_a

                stageA(0)
                for h in range(NH):
                    stageB(h)
                    if h + 1 < NH:
                        stageA(h + 1)
                    stageC(h)
                ev_y_all = ev_y[-1]
                sch.wait("v", ev_y[-2])

                yk = lambda k: acc[:, k * NT:(k + 1) * NT]
                rel["A"] = pAB_rel["A"]
                rel["B"] = pAB_rel["B"]
                ev_ab, ev_pab = emit_ln(yk, ybf, ybf, ev_y_all, "A", "B", sq2,
                                        cast_bf=True, apply=False,
                                        wait_sq_free=prev_xn2_st)
                for k in range(KP):
                    sch.op("v", lambda e, k=k: e.tensor_mul(yk(k), yk(k), ln3[:]),
                           inc="V" if k == KP - 1 else None,
                           waits=[ev_ab] if k == 0 else ())
                sch.edge("v", "V")
                for k in range(KP):
                    sch.op("v", lambda e, k=k: e.tensor_add(yk(k), yk(k), ln1[:]),
                           inc="V" if k == KP - 1 else None)
                sch.edge("v", "V")
                for k in range(KP):
                    sch.op("v", lambda e, k=k: e.tensor_scalar(
                        yk(k), yk(k), gb_sb[:, k:k + 1], gb_sb[:, 8 + k:9 + k],
                        OP.mult, OP.add),
                        inc="V" if k == KP - 1 else None,
                        waits=[ev_gb] if k == 0 else ())
                sch.edge("v", "V")
                ev_x2 = None
                for k in range(KP):
                    ev_x2 = sch.op("v", lambda e, k=k: e.tensor_add(
                        yk(k), yk(k), x_res[:, k * NT:(k + 1) * NT]),
                        inc="V" if k == KP - 1 else None,
                        waits=[ev_x1] if k == 0 else ())
                sch.edge("v", "V")
                free["x_res"] = ev_x2
                free["xn"] = ev_y_all
                ev_stx2 = sch.dma("g", featmaj(x2T)[:, :, c0:c0 + NT],
                                  acc[:].rearrange("p (k n) -> p k n", n=NT), "ST",
                                  waits=[ev_x2])
                if PHASES == "f1a":
                    sch.dma("g", featmaj(outT)[:, :, c0:c0 + NT],
                            acc[:].rearrange("p (k n) -> p k n", n=NT), "ST",
                            waits=[ev_x2])
                st_events[("x2", ti)] = ev_stx2
                prev_x2_st = ev_stx2
                ev_xn2, ev_pab2 = emit_ln(yk, ybf, ybf, ev_x2, "A", "B", sq2,
                                          cast_bf=True, apply=True,
                                          wait_dst_free=prev_xn2_st)
                pAB_rel["A"] = ev_pab2
                pAB_rel["B"] = ev_pab2
                ev_stxn2 = sch.dma(
                    "g", featmaj(xn2T)[:, :, c0:c0 + NT],
                    xs[:, 12288:12288 + KP * NT].rearrange("p (k n) -> p k n", n=NT),
                    "ST", waits=[ev_xn2])
                st_events[("xn2", ti)] = ev_stxn2
                prev_xn2_st = ev_stxn2
                free["acc"] = [ev_stx2, ("S", sch.cnt["S"]), ("T", sch.cnt["T"])]
                free_xs2 = [("T", sch.cnt["T"]), ev_stxn2]

            free["xs"] = free_xs2

        if PHASES == "f1":
            emit_ffn(1, xT, xbT, outT, "x1")
        else:
            emit_ffn(1, xT, xbT, x1T, "x1")
            emit_attn()
        bar2 = sch.barrier()
        wW_rel[0] = bar2
        wW_rel[1] = bar2
        wD_rel[0] = bar2
        wD_rel[1] = bar2
        pAB_rel["A"] = bar2
        pAB_rel["B"] = bar2
        for b in rel:
            rel[b] = bar2
        sig_rel[0] = bar2
        sig_rel[1] = bar2
        if PHASES == "all":
            emit_ffn(2, x2T, xn2T, outT, "out")

        sch.op("s", lambda e: e.activation(eps_t[:, 0:1], eps_t[:, 0:1], AF.Copy),
               waits=[("ST", sch.cnt["ST"])])
        sch.op("g", lambda e: e.engine_nop(), waits=[("AL", sch.cnt["AL"])])

        for qn in sch.lastw:
            for s, v in sch.lastw[qn].items():
                assert v <= sch.cnt[s], (qn, s, v, sch.cnt[s])

        with nc.Block() as block:
            block.tensor(lambda e: [f(e) for f in sch.q["t"]])
            block.vector(lambda e: [f(e) for f in sch.q["v"]])
            block.scalar(lambda e: [f(e) for f in sch.q["s"]])
            block.gpsimd(lambda e: [f(e) for f in sch.q["g"]])
            block.sync(lambda e: [f(e) for f in sch.q["y"]])

    return nc


def prep_inputs(L0_attn_x, L0_attn_Wq, L0_attn_bq, L0_attn_Wk, L0_attn_bk,
                L0_attn_Wv, L0_attn_bv, L0_attn_gamma, L0_attn_beta,
                L0_ffn_router_gate, L0_ffn_w_gate, L0_ffn_router_up, L0_ffn_w_up,
                L0_ffn_router_down, L0_ffn_w_down):
    f = np.float32
    x = np.asarray(L0_attn_x, f)
    B = x.shape[0]

    def blockify_gu(w):
        wt = np.asarray(w, f).transpose(0, 2, 1).reshape(E, KP, 128, MG, 128)
        return np.ascontiguousarray(wt.transpose(3, 2, 0, 1, 4))

    def blockify_d(w):
        wt = np.asarray(w, f).transpose(0, 2, 1).reshape(E, KD, 128, MD, 128)
        return np.ascontiguousarray(wt.transpose(3, 2, 0, 1, 4))

    gamma = np.asarray(L0_attn_gamma, f).reshape(KP, 128).T
    beta = np.asarray(L0_attn_beta, f).reshape(KP, 128).T
    shared = {
        "wgF": blockify_gu(L0_ffn_w_gate),
        "wuF": blockify_gu(L0_ffn_w_up),
        "wdF": blockify_d(L0_ffn_w_down),
        "wqF": np.ascontiguousarray(np.asarray(L0_attn_Wq, f).T),
        "wkF": np.ascontiguousarray(np.asarray(L0_attn_Wk, f).T),
        "wvF": np.ascontiguousarray(np.asarray(L0_attn_Wv, f).T),
        "rguF": np.ascontiguousarray(np.concatenate(
            [np.asarray(L0_ffn_router_gate, f).T,
             np.asarray(L0_ffn_router_up, f).T], axis=1)),
        "rdF": np.ascontiguousarray(np.asarray(L0_ffn_router_down, f).T),
        "browF": np.concatenate(
            [np.asarray(L0_attn_bq, f), np.asarray(L0_attn_bk, f),
             np.asarray(L0_attn_bv, f)]).reshape(1, 3 * D),
        "gbF": np.ascontiguousarray(np.concatenate([gamma, beta], axis=1)),
        "idF": np.eye(128, dtype=f),
    }
    in_maps = []
    for b in range(B):
        m = dict(shared)
        m["xT"] = np.ascontiguousarray(x[b].T)
        in_maps.append(m)
    return in_maps


def kernel(**inputs):
    in_maps = prep_inputs(**inputs)
    nc = build_program(n_tiles=T_FULL // NT)
    res = run_bass_kernel_spmd(nc, in_maps, core_ids=list(range(len(in_maps))))
    outs = [np.ascontiguousarray(np.asarray(r["outT"], np.float32).T)
            for r in res.results]
    return np.stack(outs, axis=0)


# version 12
# speedup vs baseline: 1.0083x; 1.0083x over previous
import numpy as np
from contextlib import ExitStack
import concourse.bass as bass
import concourse.mybir as mybir
from concourse.bass_utils import run_bass_kernel_spmd

F32 = mybir.dt.float32
BF16 = mybir.dt.bfloat16
AF = mybir.ActivationFunctionType
OP = mybir.AluOpType
AX = mybir.AxisListType.X

D = 1024
FF = 4096
E = 8
NH = 16
HD = 64
KP = D // 128
MG = FF // 128
MD = KP
KD = FF // 128
NT = 512
TB = NT // 128
EPS_LN = 1e-5
EPS_DEN = 1e-6
BIG = 1.0e30
N_CORES = 8
T_FULL = 8192
DBG_TAKE = 30
PHASES = "all"


class Sch:
    def __init__(self, sems):
        self.sems = sems
        self.cnt = {k: 0 for k in sems}
        self.q = {k: [] for k in "tvsgy"}
        self.lastw = {qn: {} for qn in "tvsgy"}

    def wait(self, qn, ev):
        if ev is None:
            return
        if isinstance(ev, list):
            for e in ev:
                self.wait(qn, e)
            return
        s, v = ev
        if self.lastw[qn].get(s, -1) >= v:
            return
        self.lastw[qn][s] = v
        h = self.sems[s]
        self.q[qn].append(lambda e, h=h, v=v: e.wait_ge(h, v))

    def op(self, qn, fn, inc=None, waits=(), d=1):
        for ev in waits:
            self.wait(qn, ev)
        if inc is None:
            self.q[qn].append(fn)
            return None
        self.cnt[inc] += d
        ev = (inc, self.cnt[inc])
        h = self.sems[inc]
        self.q[qn].append(lambda e, fn=fn, h=h, d=d: fn(e).then_inc(h, d))
        return ev

    def dma(self, qn, out, in_, inc, waits=()):
        ev = self.op(qn, lambda e, o=out, i=in_: e.dma_start(out=o, in_=i),
                     inc=inc, waits=waits, d=16)
        return ev

    def edge(self, qn, name):
        self.wait(qn, (name, self.cnt[name]))

    def barrier(self):
        return [(n, self.cnt[n]) for n in ("T", "V", "S", "WL", "AL", "ST", "G")
                if self.cnt[n] > 0]


def build_program(n_tiles=16):
    T = NT * n_tiles
    nc = bass.Bass("TRN2", target_bir_lowering=False)
    dtens = lambda name, shape, dty, kind: nc.dram_tensor(name, shape, dty, kind=kind).ap()

    xT = dtens("xT", [D, T], F32, "ExternalInput")
    wgF = dtens("wgF", [MG, 128, E, KP, 128], F32, "ExternalInput")
    wuF = dtens("wuF", [MG, 128, E, KP, 128], F32, "ExternalInput")
    wdF = dtens("wdF", [MD, 128, E, KD, 128], F32, "ExternalInput")
    wqF = dtens("wqF", [D, D], F32, "ExternalInput")
    wkF = dtens("wkF", [D, D], F32, "ExternalInput")
    wvF = dtens("wvF", [D, D], F32, "ExternalInput")
    rguF = dtens("rguF", [D, 16], F32, "ExternalInput")
    rdF = dtens("rdF", [FF, 8], F32, "ExternalInput")
    browF = dtens("browF", [1, 3 * D], F32, "ExternalInput")
    gbF = dtens("gbF", [128, 16], F32, "ExternalInput")
    idF = dtens("idF", [128, 128], F32, "ExternalInput")
    outT = dtens("outT", [D, T], F32, "ExternalOutput")

    wgB = dtens("wgB", [MG, 128, E, KP, 128], BF16, "Internal")
    wuB = dtens("wuB", [MG, 128, E, KP, 128], BF16, "Internal")
    wdB = dtens("wdB", [MD, 128, E, KD, 128], BF16, "Internal")
    wqB = dtens("wqB", [D, D], BF16, "Internal")
    wkB = dtens("wkB", [D, D], BF16, "Internal")
    wvB = dtens("wvB", [D, D], BF16, "Internal")
    xbT = dtens("xbT", [D, T], BF16, "Internal")
    x1T = dtens("x1T", [D, T], F32, "Internal")
    xn1T = dtens("xn1T", [D, T], BF16, "Internal")
    x2T = dtens("x2T", [D, T], F32, "Internal")
    xn2T = dtens("xn2T", [D, T], BF16, "Internal")

    def featmaj(ap_):
        return ap_.rearrange("(k p) n -> p k n", p=128)

    with ExitStack() as ctx:
        en = ctx.enter_context
        x_res = en(nc.sbuf_tensor("x_res", [128, KP * NT], F32))
        xn_b = en(nc.sbuf_tensor("xn_b", [128, KP * NT], BF16))
        xs = en(nc.sbuf_tensor("xs", [128, E * KP * NT], BF16))
        t1 = en(nc.sbuf_tensor("t1", [128, MG * NT], BF16))
        acc = en(nc.sbuf_tensor("acc", [128, KP * NT], F32))
        wW0 = en(nc.sbuf_tensor("wW0", [128, E * KP * 128], BF16))
        wW1 = en(nc.sbuf_tensor("wW1", [128, E * KP * 128], BF16))
        wD0 = en(nc.sbuf_tensor("wD0", [128, KD * 128], BF16))
        wD1 = en(nc.sbuf_tensor("wD1", [128, KD * 128], BF16))
        bcd = en(nc.sbuf_tensor("bcd", [128, 2 * NT], BF16))
        bcg = en(nc.sbuf_tensor("bcg", [128, 2 * NT], BF16))
        sigt = en(nc.sbuf_tensor("sigt", [128, 2 * NT], BF16))
        ln0 = en(nc.sbuf_tensor("ln0", [128, NT], F32))
        ln1 = en(nc.sbuf_tensor("ln1", [128, NT], F32))
        ln2 = en(nc.sbuf_tensor("ln2", [128, NT], F32))
        tmpd = en(nc.sbuf_tensor("tmpd", [128, NT], F32))
        ln3 = en(nc.sbuf_tensor("ln3", [128, NT], F32))
        kv_sb = wW1[0:64, 0:NH * HD]
        ksf = en(nc.sbuf_tensor("ksf", [128, 8], F32))
        ksb = en(nc.sbuf_tensor("ksb", [128, 8], BF16))
        KS = en(nc.sbuf_tensor("KS", [128, KP * 16], BF16))
        rtok = en(nc.sbuf_tensor("rtok", [128, TB * 16], F32))
        rtok_bf = en(nc.sbuf_tensor("rtok_bf", [128, TB * 16], BF16))
        rgu_sb = en(nc.sbuf_tensor("rgu_sb", [128, KP * 16], BF16))
        rd_sb = en(nc.sbuf_tensor("rd_sb", [128, KD * 8], BF16))
        gb_sb = en(nc.sbuf_tensor("gb_sb", [128, 16], F32))
        idb = en(nc.sbuf_tensor("idb", [128, 128], BF16))
        idf = en(nc.sbuf_tensor("idf", [128, 128], F32))
        rt_sb = ln0[0:16, :]
        ones128 = en(nc.sbuf_tensor("ones128", [128, 128], BF16))
        ones1b = en(nc.sbuf_tensor("ones1b", [1, 128], BF16))
        onescol = en(nc.sbuf_tensor("onescol", [128, 1], BF16))
        eps_t = en(nc.sbuf_tensor("eps_t", [128, 1], F32))
        comb = en(nc.sbuf_tensor("comb", [128, TB * 16], F32))
        comb_bf = en(nc.sbuf_tensor("comb_bf", [128, TB * 16], BF16))
        combd = en(nc.sbuf_tensor("combd", [128, TB * 8], F32))
        combd_bf = en(nc.sbuf_tensor("combd_bf", [128, TB * 8], BF16))
        rm1 = en(nc.sbuf_tensor("rm1", [128, 12], F32))
        rm2 = en(nc.sbuf_tensor("rm2", [128, 12], F32))
        rp1 = en(nc.sbuf_tensor("rp1", [128, 12], F32))
        rp2 = en(nc.sbuf_tensor("rp2", [128, 12], F32))
        rmsk = en(nc.sbuf_tensor("rmsk", [128, TB * 24], F32))
        rl2 = en(nc.sbuf_tensor("rl2", [128, TB * 24], F32))
        crow = en(nc.sbuf_tensor("crow", [1, 2 * NT], BF16))
        brow_sb = xs[0:1, 25600:25600 + 3 * D]

        pA = en(nc.psum_tensor("pA", [128, 512], F32))
        pB = en(nc.psum_tensor("pB", [128, 512], F32))
        pC = en(nc.psum_tensor("pC", [128, 512], F32))
        pD = en(nc.psum_tensor("pD", [128, 512], F32))
        pE = en(nc.psum_tensor("pE", [128, 512], F32))
        pF = en(nc.psum_tensor("pF", [128, 1024], BF16))
        pG = en(nc.psum_tensor("pG", [128, 512], F32))
        pH = en(nc.psum_tensor("pH", [128, 512], F32))

        sems = {n: en(nc.semaphore(n)) for n in ("T", "V", "S", "G", "WL", "WL2", "WL3", "WL4", "AL", "AL2", "ST")}
        sch = Sch(sems)

        rel = {b: None for b in ("A", "B", "C", "D", "E", "F", "G", "H")}
        bank = {"A": pA, "B": pB, "C": pC, "D": pD, "E": pE, "F": pF, "G": pG, "H": pH}

        ev_ones = sch.op("g", lambda e: e.memset(ones128[:], 1.0), inc="G")
        ev_ones1b = sch.op("g", lambda e: e.memset(ones1b[:], 1.0), inc="G")
        ev_onescol = sch.op("g", lambda e: e.memset(onescol[:], 1.0), inc="G")
        ev_ksz = sch.op("g", lambda e: e.memset(KS[:], 0.0), inc="G")
        ev_eps = sch.op("g", lambda e: e.memset(eps_t[:], EPS_LN), inc="G")
        ev_gb = sch.dma("g", gb_sb[:], gbF, "AL")
        ev_idb = sch.dma("g", idb[:], idF, "AL")
        ev_idf = sch.dma("g", idf[:], idF, "AL")
        ev_rgu = sch.dma("g", rgu_sb[:].rearrange("p (k c) -> p k c", c=16),
                         rguF.rearrange("(k p) c -> p k c", p=128), "AL")
        ev_rd = sch.dma("g", rd_sb[:].rearrange("p (k c) -> p k c", c=8),
                        rdF.rearrange("(k p) c -> p k c", p=128), "AL")
        ev_wg = [sch.dma("g", wgB[c:c + 4], wgF[c:c + 4], "AL") for c in range(0, MG, 4)]
        ev_wu = [sch.dma("g", wuB[c:c + 4], wuF[c:c + 4], "AL") for c in range(0, MG, 4)]
        ev_wd = [sch.dma("g", wdB[c:c + 1], wdF[c:c + 1], "AL") for c in range(MD)]
        ev_xbt = sch.dma("g", xbT, xT, "AL")
        ev_wq = sch.dma("g", wqB, wqF, "AL")
        ev_wk = sch.dma("g", wkB, wkF, "AL")
        ev_wv = sch.dma("g", wvB, wvF, "AL")

        def emit_ln(src_f32, src_bf, dst_bf, wait_src, b1n, b2n, sq_of,
                    cast_bf=False, apply=True, wait_dst_free=None, wait_sq_free=None):
            b1, b2 = bank[b1n], bank[b2n]
            if cast_bf:
                ev_cast = None
                for k in range(KP):
                    ev_cast = sch.op("s", lambda e, k=k: e.activation(
                        src_bf(k), src_f32(k), AF.Copy), inc="S",
                        waits=([wait_src, wait_sq_free] if k == 0 else ()))
            else:
                ev_cast = wait_src
            ev_sq = None
            for k in range(KP):
                ev_sq = sch.op("s", lambda e, k=k: e.activation(
                    sq_of(k), src_f32(k), AF.Square), inc="S",
                    waits=([wait_src, wait_sq_free] if k == 0 else ()))
            ev_s1 = None
            for k in range(KP):
                ev_s1 = sch.op("t", lambda e, k=k, b1=b1: e.matmul(
                    b1[:], ones128[:], src_bf(k), start=(k == 0), stop=(k == KP - 1)),
                    inc="T" if k == KP - 1 else None,
                    waits=([ev_cast if cast_bf else wait_src, ev_ones, rel[b1n]]
                           if k == 0 else ()))
            ev_s2 = None
            for k in range(KP):
                ev_s2 = sch.op("t", lambda e, k=k, b2=b2: e.matmul(
                    b2[:], ones128[:], sq_of(k), start=(k == 0), stop=(k == KP - 1)),
                    inc="T" if k == KP - 1 else None,
                    waits=([ev_sq, rel[b2n]] if k == 0 else ()))
            c1 = 1.0 / D
            sch.op("v", lambda e: e.tensor_scalar_mul(ln0[:], b1[:], c1),
                   inc="V", waits=[ev_s1])
            ev_bfree = sch.op("v", lambda e: e.tensor_scalar_mul(ln1[:], b2[:], c1),
                              inc="V", waits=[ev_s2])
            rel[b1n] = ev_bfree
            rel[b2n] = ev_bfree
            sch.edge("v", "V")
            sch.op("v", lambda e: e.tensor_mul(ln2[:], ln0[:], ln0[:]), inc="V")
            sch.edge("v", "V")
            ev = sch.op("v", lambda e: e.tensor_sub(ln1[:], ln1[:], ln2[:]), inc="V")
            ev = sch.op("s", lambda e: e.activation(ln2[:], ln1[:], AF.Sqrt,
                                                    bias=eps_t[:, 0:1]),
                        inc="S", waits=[ev, ev_eps])
            ev = sch.op("v", lambda e: e.reciprocal(ln3[:], ln2[:]), inc="V", waits=[ev])
            sch.edge("v", "V")
            ev_ab = sch.op("v", lambda e: e.scalar_tensor_tensor(
                ln1[:], ln0[:], -1.0, ln3[:], OP.mult, OP.mult), inc="V")
            sch.edge("v", "V")
            if not apply:
                return ev_ab, ev_bfree
            for k in range(KP):
                sch.op("v", lambda e, k=k: e.tensor_mul(dst_bf(k), src_f32(k), ln3[:]),
                       inc="V" if k == KP - 1 else None,
                       waits=([wait_dst_free] if (k == 0 and wait_dst_free) else ()))
            sch.edge("v", "V")
            ev_ap = None
            for k in range(KP):
                ev_ap = sch.op("v", lambda e, k=k: e.tensor_add(dst_bf(k), dst_bf(k), ln1[:]),
                               inc="V" if k == KP - 1 else None)
            sch.edge("v", "V")
            return ev_ap, ev_bfree

        def emit_top2(psrc, n_r, combo, combo_bf, moff, mskoff, wait_logits):
            for j in range(n_r):
                sch.op("v", lambda e, j=j: e.reduce_max(
                    rm1[:, moff + j:moff + j + 1], psrc(j), axis=AX),
                    inc="V" if j == n_r - 1 else None,
                    waits=[wait_logits] if j == 0 else ())
            sch.edge("v", "V")
            for j in range(n_r):
                sch.op("v", lambda e, j=j: e.tensor_scalar(
                    rmsk[:, mskoff + j * 8: mskoff + (j + 1) * 8], psrc(j),
                    rm1[:, moff + j:moff + j + 1], None, OP.is_equal),
                    inc="V" if j == n_r - 1 else None)
            sch.edge("v", "V")
            for j in range(n_r):
                sch.op("v", lambda e, j=j: e.scalar_tensor_tensor(
                    rl2[:, mskoff + j * 8: mskoff + (j + 1) * 8],
                    rmsk[:, mskoff + j * 8: mskoff + (j + 1) * 8], -BIG,
                    psrc(j), OP.mult, OP.add),
                    inc="V" if j == n_r - 1 else None)
            sch.edge("v", "V")
            for j in range(n_r):
                sch.op("v", lambda e, j=j: e.reduce_max(
                    rm2[:, moff + j:moff + j + 1],
                    rl2[:, mskoff + j * 8: mskoff + (j + 1) * 8], axis=AX),
                    inc="V" if j == n_r - 1 else None)
            sch.edge("v", "V")
            ev = sch.op("v", lambda e: e.tensor_sub(
                rp1[:, moff:moff + n_r], rm1[:, moff:moff + n_r],
                rm2[:, moff:moff + n_r]), inc="V")
            ev = sch.op("s", lambda e: e.activation(
                rp1[:, moff:moff + n_r], rp1[:, moff:moff + n_r], AF.Sigmoid),
                inc="S", waits=[ev])
            ev = sch.op("v", lambda e: e.tensor_scalar(
                rp2[:, moff:moff + n_r], rp1[:, moff:moff + n_r], -1.0, 1.0,
                OP.mult, OP.add), inc="V", waits=[ev])
            sch.edge("v", "V")
            for j in range(n_r):
                sch.op("v", lambda e, j=j: e.tensor_scalar(
                    combo[:, j * 8:(j + 1) * 8],
                    rmsk[:, mskoff + j * 8: mskoff + (j + 1) * 8],
                    rp1[:, moff + j:moff + j + 1], None, OP.mult),
                    inc="V" if j == n_r - 1 else None)
            sch.edge("v", "V")
            for j in range(n_r):
                sch.op("v", lambda e, j=j: e.tensor_scalar(
                    rmsk[:, mskoff + j * 8: mskoff + (j + 1) * 8],
                    rl2[:, mskoff + j * 8: mskoff + (j + 1) * 8],
                    rm2[:, moff + j:moff + j + 1], None, OP.is_equal),
                    inc="V" if j == n_r - 1 else None)
            sch.edge("v", "V")
            for j in range(n_r):
                sch.op("v", lambda e, j=j: e.scalar_tensor_tensor(
                    combo[:, j * 8:(j + 1) * 8],
                    rmsk[:, mskoff + j * 8: mskoff + (j + 1) * 8],
                    rp2[:, moff + j:moff + j + 1],
                    combo[:, j * 8:(j + 1) * 8], OP.mult, OP.add),
                    inc="V" if j == n_r - 1 else None)
            sch.edge("v", "V")
            ev_bf = sch.op("v", lambda e: e.tensor_copy(combo_bf[:], combo[:]), inc="V")
            sch.edge("v", "V")
            return ev_bf

        pF_rel = [None]
        crow_rel = [None, None]
        bcg_rel = [None, None]
        bcd_rel = [None, None]

        def emit_bcast(col_of, slot, wait_src, dst_bf=None, dst_rel=None,
                       extra_tr_waits=()):
            half = slot
            ev_tr = None
            for tb in range(TB):
                ev_tr = sch.op("t", lambda e, tb=tb, half=half: e.transpose(
                    pF[0:1, half * 512 + tb * 128: half * 512 + tb * 128 + 128],
                    col_of(tb), idb[:]),
                    inc="T" if tb == TB - 1 else None,
                    waits=([wait_src, ev_idb, pF_rel[0]] + list(extra_tr_waits))
                    if tb == 0 else ())
            ev_cr = sch.op("v", lambda e, half=half: e.tensor_copy(
                crow[0:1, half * NT:(half + 1) * NT],
                pF[0:1, half * 512:(half + 1) * 512]), inc="V",
                waits=[ev_tr, crow_rel[half]])
            ev_bc = sch.op("t", lambda e: e.matmul(
                pH[:], ones1b[:],
                crow[0:1, half * NT:(half + 1) * NT],
                start=True, stop=True),
                inc="T", waits=[ev_cr, ev_ones1b, rel["H"]])
            crow_rel[half] = ev_bc
            pF_rel[0] = ev_cr
            ev = sch.op("s", lambda e, d=dst_bf: e.activation(
                d, pH[:], AF.Copy), inc="S", waits=[ev_bc, dst_rel])
            rel["H"] = ev
            return ev


        def bcast_steps(col_of, slot, wait_src_getter, dst_bf, dst_rel_getter,
                        done_cb):
            st = {}
            def s1():
                half = slot
                ev_tr = None
                for tb in range(TB):
                    ev_tr = sch.op("t", lambda e, tb=tb, half=half: e.transpose(
                        pF[0:1, half * 512 + tb * 128: half * 512 + tb * 128 + 128],
                        col_of(tb), idb[:]),
                        inc="T" if tb == TB - 1 else None,
                        waits=([wait_src_getter(), ev_idb, pF_rel[0]]) if tb == 0 else ())
                st["ev_tr"] = ev_tr
            def s2():
                half = slot
                ev_cr = sch.op("v", lambda e, half=half: e.tensor_copy(
                    crow[0:1, half * NT:(half + 1) * NT],
                    pF[0:1, half * 512:(half + 1) * 512]), inc="V",
                    waits=[st["ev_tr"], crow_rel[half]])
                pF_rel[0] = ev_cr
                st["ev_cr"] = ev_cr
            def s3():
                half = slot
                ev_bc = sch.op("t", lambda e, half=half: e.matmul(
                    pH[:], ones1b[:],
                    crow[0:1, half * NT:(half + 1) * NT],
                    start=True, stop=True),
                    inc="T", waits=[st["ev_cr"], ev_ones1b, rel["H"]])
                crow_rel[half] = ev_bc
                ev = sch.op("v", lambda e, d=dst_bf: e.tensor_copy(
                    d, pH[:]), inc="V", waits=[ev_bc, dst_rel_getter()])
                rel["H"] = ev
                done_cb(ev)
            return [s1, s2, s3]

        st_events = {}

        free = {"x_res": None, "xn": None, "xs": None, "acc": None}
        sig_rel = [None, None]
        pAB_rel = rel
        CYC4 = ("A", "B", "G")
        wW_rel = [None, None]
        wD_rel = [None, None]

        def emit_ffn(inv, srcT, src_bfT, dstT, dst_name):
            def make_head(ti):
                ctx = {}
                c0 = ti * NT

                def preA():
                    w_x = [free["x_res"]]
                    w_xn = [free["xn"]]
                    if inv == 2:
                        w_x.append(st_events.get(("x2", ti)))
                        w_xn.append(st_events.get(("xn2", ti)))
                    else:
                        w_xn.append(ev_xbt)
                    sch.dma(
                        "s", x_res[:].rearrange("p (k n) -> p k n", n=NT),
                        featmaj(srcT)[:, :, c0:c0 + NT], "AL2", waits=w_x)
                    ev_pair = sch.dma(
                        "s", xn_b[:].rearrange("p (k n) -> p k n", n=NT),
                        featmaj(src_bfT)[:, :, c0:c0 + NT], "AL2", waits=w_xn)
                    ctx["ev_x"] = ev_pair
                    ctx["ev_xn"] = ev_pair

                def preLN():
                    xr = lambda k: x_res[:, k * NT:(k + 1) * NT]
                    xnk = lambda k: xn_b[:, k * NT:(k + 1) * NT]
                    sqk = lambda k: xs[:, k * NT:(k + 1) * NT]
                    if inv == 1:
                        ev_ln, _ = emit_ln(xr, xnk, xnk, [ctx["ev_xn"], ctx["ev_x"]],
                                           "C", "D", sqk, wait_sq_free=free["xs"])
                        ctx["ev_sq_last"] = ("S", sch.cnt["S"])
                        ctx["ev_apply1"] = ("V", sch.cnt["V"])
                        ctx["ev_xn_ready"] = ev_ln
                    else:
                        ctx["ev_sq_last"] = None
                        ctx["ev_apply1"] = None
                        ctx["ev_xn_ready"] = ctx["ev_xn"]

                def preB():
                    ev_rt = None
                    for k in range(KP):
                        ev_rt = sch.op("t", lambda e, k=k: e.matmul(
                            pH[0:16, 0:NT], rgu_sb[:, k * 16:(k + 1) * 16],
                            xn_b[:, k * NT:(k + 1) * NT],
                            start=(k == 0), stop=(k == KP - 1),
                            skip_group_check=True),
                            inc="T" if k == KP - 1 else None,
                            waits=([ctx["ev_xn_ready"], ev_rgu, rel["H"]]
                                   if k == 0 else ()))
                    ev_rte = sch.op("v", lambda e: e.tensor_copy(
                        rt_sb[:], pH[0:16, 0:NT]), inc="V", waits=[ev_rt])
                    rel["H"] = ev_rte
                    ev_rtt = None
                    for tb in range(TB):
                        ev_rtt = sch.op("t", lambda e, tb=tb: e.transpose(
                            pE[:, tb * 16:(tb + 1) * 16],
                            rt_sb[0:16, tb * 128:(tb + 1) * 128], idf[0:16, 0:16]),
                            inc="T" if tb == TB - 1 else None,
                            waits=[ev_rte, ev_idf, rel["E"]] if tb == 0 else ())
                    ctx["ev_comb"] = emit_top2(
                        lambda j: pE[:, (j // 2) * 16 + (j % 2) * 8:
                                     (j // 2) * 16 + (j % 2) * 8 + 8],
                        TB * 2, comb, comb_bf, 0, 0, ev_rtt)

                def xs_steps(ei):
                    slot = ei % 2
                    hstate = {}
                    steps = bcast_steps(
                        lambda tb, ei=ei: comb_bf[:, tb * 16 + ei:tb * 16 + ei + 1],
                        slot, lambda: ctx["ev_comb"],
                        bcg[:, slot * NT:(slot + 1) * NT],
                        lambda slot=slot: bcg_rel[slot],
                        lambda ev: hstate.update(ev_cp=ev))
                    def tts():
                        last = None
                        for k in range(KP):
                            last = sch.op("v", lambda e, ei=ei, k=k, slot=slot: e.tensor_tensor(
                                xs[:, (ei * KP + k) * NT:(ei * KP + k + 1) * NT],
                                xn_b[:, k * NT:(k + 1) * NT],
                                bcg[:, slot * NT:(slot + 1) * NT], OP.mult),
                                inc="V" if k == KP - 1 else None,
                                waits=[hstate["ev_cp"], ctx["ev_xn_ready"], free["xs"]]
                                if k == 0 else ())
                        bcg_rel[slot] = last
                        ctx["ev_xs_gate"] = last
                    return steps + [tts]
                hsteps = [preA, preLN, preB]
                for ei in range(E):
                    hsteps.extend(xs_steps(ei))
                return ctx, hsteps

            ctx0, thunks0 = make_head(0)
            for th in thunks0:
                th()
            heads = {0: ctx0}

            for ti in range(n_tiles):
                c0 = ti * NT
                ctx = heads.pop(ti)
                ev_init = None
                for k in range(KP):
                    ev_init = sch.op("s", lambda e, k=k: e.activation(
                        acc[:, k * NT:(k + 1) * NT], x_res[:, k * NT:(k + 1) * NT],
                        AF.Copy, scale=2.0), inc="S" if k == KP - 1 else None,
                        waits=[free["acc"], ctx["ev_x"]] if k == 0 else ())
                free["x_res"] = [ev for ev in (ev_init, ctx["ev_sq_last"],
                                               ctx["ev_apply1"]) if ev]

                ev_gate = [None] * MG
                for m in range(MG):
                    buf = wW0 if m % 2 == 0 else wW1
                    bn = CYC4[m % 3]
                    bk = bank[bn]
                    ev_ld = sch.dma("y", buf[:].rearrange("p (e k c) -> p e k c", k=KP, c=128),
                                    wgB[m], "WL",
                                    waits=[ev_wg[m // 4], wW_rel[m % 2]])
                    w0 = [ev_ld, pAB_rel[bn]] + ([ctx["ev_xs_gate"]] if m == 0 else [])
                    last = None
                    for ei in range(E):
                        for k in range(KP):
                            last = sch.op("t", lambda e, ei=ei, k=k, buf=buf, bk=bk: e.matmul(
                                bk[:], buf[:, (ei * KP + k) * 128:(ei * KP + k + 1) * 128],
                                xs[:, (ei * KP + k) * NT:(ei * KP + k + 1) * NT],
                                start=(ei == 0 and k == 0),
                                stop=(ei == E - 1 and k == KP - 1)),
                                inc="T" if (ei == E - 1 and k == KP - 1) else None,
                                waits=w0 if (ei == 0 and k == 0) else ())
                    ev_gate[m] = last
                    wW_rel[m % 2] = last
                    ev_sg = sch.op("s", lambda e, m=m, bk=bk: e.activation(
                        sigt[:, (m % 2) * NT:(m % 2 + 1) * NT], bk[:], AF.Sigmoid),
                        inc="S", waits=[last, sig_rel[m % 2]])
                    ev_t1 = sch.op("v", lambda e, m=m, bk=bk: e.tensor_tensor(
                        t1[:, m * NT:(m + 1) * NT],
                        sigt[:, (m % 2) * NT:(m % 2 + 1) * NT], bk[:], OP.mult),
                        inc="V", waits=[ev_sg])
                    sig_rel[m % 2] = ev_t1
                    pAB_rel[bn] = ev_t1

                ev_xs_up = None
                for ei in range(E):
                    slot = ei % 2
                    ev_cp = emit_bcast(
                        lambda tb, ei=ei: comb_bf[:, tb * 16 + 8 + ei:tb * 16 + 8 + ei + 1],
                        slot, ctx["ev_comb"], dst_bf=bcg[:, slot * NT:(slot + 1) * NT],
                        dst_rel=bcg_rel[slot])
                    for k in range(KP):
                        ev_xs_up = sch.op("v", lambda e, ei=ei, k=k, slot=slot: e.tensor_tensor(
                            xs[:, (ei * KP + k) * NT:(ei * KP + k + 1) * NT],
                            xn_b[:, k * NT:(k + 1) * NT],
                            bcg[:, slot * NT:(slot + 1) * NT], OP.mult),
                            inc="V" if k == KP - 1 else None,
                            waits=[ev_cp, ev_gate[MG - 1]] if k == 0 else ())
                    bcg_rel[slot] = ev_xs_up
                free["xn"] = ev_xs_up

                ev_up = [None] * MG
                ev_drt = None
                for m in range(MG):
                    buf = wW0 if m % 2 == 0 else wW1
                    bn = CYC4[m % 3]
                    bk = bank[bn]
                    ev_ld = sch.dma("y", buf[:].rearrange("p (e k c) -> p e k c", k=KP, c=128),
                                    wuB[m], "WL", waits=[ev_wu[m // 4], wW_rel[m % 2]])
                    w0 = [ev_ld, pAB_rel[bn]] + ([ev_xs_up] if m == 0 else [])
                    last = None
                    for ei in range(E):
                        for k in range(KP):
                            last = sch.op("t", lambda e, ei=ei, k=k, buf=buf, bk=bk: e.matmul(
                                bk[:], buf[:, (ei * KP + k) * 128:(ei * KP + k + 1) * 128],
                                xs[:, (ei * KP + k) * NT:(ei * KP + k + 1) * NT],
                                start=(ei == 0 and k == 0),
                                stop=(ei == E - 1 and k == KP - 1)),
                                inc="T" if (ei == E - 1 and k == KP - 1) else None,
                                waits=w0 if (ei == 0 and k == 0) else ())
                    ev_up[m] = last
                    wW_rel[m % 2] = last
                    ev_int = sch.op("v", lambda e, m=m, bk=bk: e.tensor_tensor(
                        t1[:, m * NT:(m + 1) * NT],
                        t1[:, m * NT:(m + 1) * NT], bk[:], OP.mult),
                        inc="V", waits=[last])
                    pAB_rel[bn] = ev_int
                    for tb in range(TB):
                        ev_drt = sch.op("t", lambda e, m=m, tb=tb: e.matmul(
                            pE[:, 64 + tb * 8: 64 + tb * 8 + 8],
                            t1[:, m * NT + tb * 128: m * NT + tb * 128 + 128],
                            rd_sb[:, m * 8:(m + 1) * 8],
                            start=False, stop=(m == MG - 1),
                            skip_group_check=True),
                            inc="T" if (m == MG - 1 and tb == TB - 1) else None,
                            waits=([ev_int, ev_rd] +
                                   ([ctx["ev_comb"]] if m == 0 else [])) if tb == 0 else ())
                free["xs"] = ev_up[MG - 1]

                ev_comb_d = emit_top2(
                    lambda j: pE[:, 64 + j * 8: 64 + j * 8 + 8],
                    TB, combd, combd_bf, 8, TB * 16, ev_drt)
                rel["E"] = ev_comb_d

                if ti + 1 < n_tiles:
                    nctx, nthunks = make_head(ti + 1)
                    heads[ti + 1] = nctx
                    nthunks[0]()
                    nthunks = nthunks[1:]
                else:
                    nthunks = []
                micro = []
                sch._hd_taken = 0

                ev_res = None
                bcd_ev = {}
                def bcd_steps_for(ei):
                    slot = ei % 2
                    return bcast_steps(
                        lambda tb, ei=ei: combd_bf[:, tb * 8 + ei:tb * 8 + ei + 1],
                        slot, lambda: ev_comb_d,
                        bcd[:, slot * NT:(slot + 1) * NT],
                        lambda slot=slot: bcd_rel[slot],
                        lambda ev, ei=ei: bcd_ev.__setitem__(ei, ev))
                for stp in bcd_steps_for(0):
                    stp()
                for ei in range(E):
                    slot = ei % 2
                    if ei + 1 < E:
                        micro.extend(bcd_steps_for(ei + 1))
                    cap = DBG_TAKE if isinstance(DBG_TAKE, int) else (99 if DBG_TAKE else 0)
                    already = getattr(sch, "_hd_taken", 0)
                    take = min(5, len(nthunks), max(0, cap - already))
                    sch._hd_taken = already + take
                    micro.extend(nthunks[:take])
                    nthunks = nthunks[take:]
                    ev_bcd = bcd_ev[ei]
                    for m2 in range(MD):
                        bslot = (ei * MD + m2) % 2
                        buf = wD0 if bslot == 0 else wD1
                        bn = CYC4[(ei * MD + m2) % 3]
                        bk = bank[bn]
                        ev_ld = sch.dma("y", buf[:].rearrange("p (k c) -> p k c", c=128),
                                        wdB[m2, :, ei], "WL",
                                        waits=[ev_wd[m2], wD_rel[bslot]])
                        w0 = [ev_ld, pAB_rel[bn], ev_bcd]
                        if ei == 0 and m2 == 0:
                            w0.append(ev_init)
                        last = None
                        for k in range(KD):
                            last = sch.op("t", lambda e, k=k, buf=buf, bk=bk: e.matmul(
                                bk[:], buf[:, k * 128:(k + 1) * 128],
                                t1[:, k * NT:(k + 1) * NT],
                                start=(k == 0), stop=(k == KD - 1)),
                                inc="T" if k == KD - 1 else None,
                                waits=w0 if k == 0 else ())
                        wD_rel[bslot] = last
                        ev_t = sch.op("v", lambda e, bk=bk, slot=slot: e.tensor_tensor(
                            tmpd[:], bk[:], bcd[:, slot * NT:(slot + 1) * NT],
                            OP.mult), inc="V", waits=[last])
                        sch.edge("v", "V")
                        ev_ev = sch.op("v", lambda e, m2=m2: e.tensor_add(
                            acc[:, m2 * NT:(m2 + 1) * NT],
                            acc[:, m2 * NT:(m2 + 1) * NT], tmpd[:]), inc="V")
                        pAB_rel[bn] = ev_ev
                        bcd_rel[slot] = ev_t
                        if ei == E - 1:
                            sch.edge("v", "V")
                            ev_res = sch.op("v", lambda e, m2=m2: e.tensor_scalar_mul(
                                acc[:, m2 * NT:(m2 + 1) * NT],
                                acc[:, m2 * NT:(m2 + 1) * NT], 0.5), inc="V")
                        if micro:
                            micro.pop(0)()
                for th in micro:
                    th()
                for th in nthunks:
                    th()
                ev_st = sch.dma("s", featmaj(dstT)[:, :, c0:c0 + NT],
                                acc[:].rearrange("p (k n) -> p k n", n=NT), "ST",
                                waits=[ev_res])
                st_events[(dst_name, ti)] = ev_st
                free["acc"] = ev_st

        def emit_attn():
            bar = sch.barrier()
            ev_lwq = sch.dma("g", t1[:, 0:KP * 1024].rearrange("p (k c) -> p k c", c=1024),
                             featmaj(wqB), "AL", waits=[ev_wq] + bar)
            ev_lwk = sch.dma("g", t1[:, KP * 1024:2 * KP * 1024].rearrange(
                "p (k c) -> p k c", c=1024), featmaj(wkB), "AL", waits=[ev_wk])
            ev_lwv = sch.dma("g", wW0[:].rearrange("p (k c) -> p k c", c=1024),
                             featmaj(wvB), "AL", waits=[ev_wv])
            ev_brow = sch.dma("g", brow_sb, browF, "AL")
            kq = lambda k: t1[:, k * 1024:(k + 1) * 1024]
            kk = lambda k: t1[:, (KP + k) * 1024:(KP + k + 1) * 1024]
            kvw = lambda k: wW0[:, k * 1024:(k + 1) * 1024]
            ktok = lambda tb: xs[:, tb * 1024:(tb + 1) * 1024]
            vtok = lambda tb: xs[:, (TB + tb) * 1024:(TB + tb + 1) * 1024]
            sq_at = lambda k: xs[:, 8 * 1024 + k * NT: 8 * 1024 + (k + 1) * NT]

            def xnp(ti):
                return xn_b if ti % 2 == 0 else xs[:, 12288:16384]

            p1_free_buf = {0: free["xn"], 1: free["xs"]}
            p1_free_x = [free["x_res"]]
            p1_sq_free = [free["xs"]]
            p1ctx = {}

            def p1_head(ti):
                c0 = ti * NT
                buf = xnp(ti)
                ev_x = sch.dma("g", x_res[:].rearrange("p (k n) -> p k n", n=NT),
                               featmaj(x1T)[:, :, c0:c0 + NT], "AL",
                               waits=[p1_free_x[0], st_events[("x1", ti)]])
                ev_xr = sch.dma("g", buf.rearrange("p (k n) -> p k n", n=NT),
                                featmaj(x1T)[:, :, c0:c0 + NT], "AL",
                                waits=[p1_free_buf[ti % 2], st_events[("x1", ti)]])
                xr = lambda k: x_res[:, k * NT:(k + 1) * NT]
                xnk = lambda k, buf=buf: buf[:, k * NT:(k + 1) * NT]
                ev_ln, _ = emit_ln(xr, xnk, xnk, [ev_xr, ev_x], "G", "H", sq_at,
                                   wait_sq_free=p1_sq_free[0])
                p1_sq_free[0] = ("T", sch.cnt["T"])
                p1_free_x[0] = [ev_ln, ("S", sch.cnt["S"])]
                ev_stxn = sch.dma("g", featmaj(xn1T)[:, :, c0:c0 + NT],
                                  buf.rearrange("p (k n) -> p k n", n=NT), "ST",
                                  waits=[ev_ln])
                st_events[("xn1", ti)] = ev_stxn
                if PHASES == "p1":
                    sch.dma("g", featmaj(outT)[:, :, c0:c0 + NT],
                            buf.rearrange("p (k n) -> p k n", n=NT), "ST",
                            waits=[ev_ln])
                p1ctx[ti] = ev_ln

            def p1_proj(ti, which):
                buf = xnp(ti)
                ev_ln = p1ctx[ti]
                wt, boff, ev_lw = (kk, D, ev_lwk) if which == "k" else (kvw, 2 * D, ev_lwv)
                out = {}
                for tb in range(TB):
                    for hf in range(2):
                        bn = "A" if (tb * 2 + hf) % 2 == 0 else "B"
                        bk = bank[bn]
                        w0 = [ev_ln, ev_lw, pAB_rel[bn], ev_brow, ev_ones1b]
                        for k in range(KP):
                            sch.op("t", lambda e, tb=tb, hf=hf, k=k, bk=bk, wt=wt, buf=buf: e.matmul(
                                bk[:],
                                buf[:, k * NT + tb * 128: k * NT + tb * 128 + 128],
                                wt(k)[:, hf * 512:(hf + 1) * 512],
                                start=(k == 0), stop=False),
                                waits=w0 if k == 0 else ())
                        last = sch.op("t", lambda e, hf=hf, bk=bk, boff=boff: e.matmul(
                            bk[:], ones1b[:],
                            xs[0:1, 25600 + boff + hf * 512: 25600 + boff + (hf + 1) * 512],
                            start=False, stop=True), inc="T")
                        dst = (ktok if which == "k" else vtok)(tb)
                        if which == "k":
                            ev_e = sch.op("v", lambda e, dst=dst, hf=hf, bk=bk: e.tensor_relu(
                                dst[:, hf * 512:(hf + 1) * 512], bk[:]),
                                inc="V", waits=[last])
                        else:
                            ev_e = sch.op("v", lambda e, dst=dst, hf=hf, bk=bk: e.tensor_copy(
                                dst[:, hf * 512:(hf + 1) * 512], bk[:]),
                                inc="V", waits=[last])
                        pAB_rel[bn] = ev_e
                        out[tb] = ev_e
                return out

            p1_head(0)
            for ti in range(n_tiles):
                evk_k = p1_proj(ti, "k")
                if ti + 1 < n_tiles:
                    p1_head(ti + 1)
                evk_v = p1_proj(ti, "v")
                p1_free_buf[ti % 2] = [("T", sch.cnt["T"]), st_events[("xn1", ti)]]
                for tb in range(TB):
                    w_kv = [evk_k[tb], evk_v[tb]]
                    if ti == 0 and tb == 0:
                        w_kv += [rel["C"], rel["D"]]
                    for h in range(NH):
                        bn = "C" if h < 8 else "D"
                        bk = bank[bn]
                        hh = h % 8
                        st = (ti == 0 and tb == 0 and hh == 0)
                        ev_kv = sch.op("t", lambda e, tb=tb, h=h, hh=hh, bk=bk, st=st: e.matmul(
                            bk[0:64, hh * 64:(hh + 1) * 64],
                            ktok(tb)[:, h * HD:(h + 1) * HD],
                            vtok(tb)[:, h * HD:(h + 1) * HD],
                            start=st, stop=(ti == n_tiles - 1 and tb == TB - 1),
                            skip_group_check=True),
                            inc="T" if (ti == n_tiles - 1 and tb == TB - 1 and h == NH - 1) else None,
                            waits=w_kv if h == 0 else ())
                    for mm in range(KP):
                        st = (ti == 0 and tb == 0 and mm == 0)
                        ev_ks = sch.op("t", lambda e, tb=tb, mm=mm, st=st: e.matmul(
                            pE[:, mm:mm + 1],
                            ktok(tb)[:, mm * 128:(mm + 1) * 128], onescol[:],
                            start=st, stop=(ti == n_tiles - 1 and tb == TB - 1),
                            skip_group_check=True),
                            inc="T" if (ti == n_tiles - 1 and tb == TB - 1 and mm == KP - 1) else None,
                            waits=([ev_onescol, rel["E"]] if st else ()))
            free["xn"] = p1_free_buf[0] if (n_tiles - 1) % 2 == 1 else p1_free_buf[1]
            free["x_res"] = p1_free_x[0]

            ev_kv_done = ("T", sch.cnt["T"])
            ev1 = sch.op("v", lambda e: e.tensor_copy(wW1[0:64, 0:512], pC[0:64, :]),
                         inc="V", waits=[ev_kv_done])
            ev2 = sch.op("v", lambda e: e.tensor_copy(wW1[0:64, 512:1024], pD[0:64, :]),
                         inc="V")
            ev3 = sch.op("v", lambda e: e.tensor_copy(ksf[:], pE[:, 0:8]), inc="V")
            sch.edge("v", "V")
            ev4 = sch.op("v", lambda e: e.tensor_copy(ksb[:], ksf[:]), inc="V")
            sch.edge("v", "V")
            evK = None
            for k in range(KP):
                evK = sch.op("v", lambda e, k=k: e.tensor_copy(
                    KS[0:64, k * 16 + 2 * k: k * 16 + 2 * k + 1], ksb[0:64, k:k + 1]),
                    inc="V" if k == KP - 1 else None,
                    waits=[ev_ksz] if k == 0 else ())
            for k in range(KP):
                evK = sch.op("v", lambda e, k=k: e.tensor_copy(
                    KS[64:128, k * 16 + 2 * k + 1: k * 16 + 2 * k + 2],
                    ksb[64:128, k:k + 1]),
                    inc="V" if k == KP - 1 else None)
            sch.edge("v", "V")
            rel["C"] = ev1
            rel["D"] = ev2
            rel["E"] = ev3
            if PHASES == "p1":
                return

            qtok = lambda tb: xs[:, tb * 1024:(tb + 1) * 1024]
            qTs = lambda p: xs[:, 4096 + p * NT: 4096 + (p + 1) * NT]
            sq2 = lambda k: xs[:, 8192 + k * NT: 8192 + (k + 1) * NT]
            ybf = lambda k: xs[:, 12288 + k * NT: 12288 + (k + 1) * NT]
            qsh = lambda h: xs[0:64, 16384 + h * NT: 16384 + (h + 1) * NT]
            rbc = lambda sl: xs[0:64, 16384 + (16 + sl) * NT: 16384 + (17 + sl) * NT]
            free_xs2 = ("T", sch.cnt["T"])
            prev_xn2_st = None
            prev_x2_st = None

            for ti in range(n_tiles):
                c0 = ti * NT
                sch.dma("g", xn_b[:].rearrange("p (k n) -> p k n", n=NT),
                                featmaj(xn1T)[:, :, c0:c0 + NT], "AL",
                                waits=[free["xn"], st_events[("xn1", ti)]])
                ev_x1 = sch.dma("g", x_res[:].rearrange("p (k n) -> p k n", n=NT),
                                featmaj(x1T)[:, :, c0:c0 + NT], "AL",
                                waits=[free["x_res"]])
                ev_xn1 = ev_x1
                ev_q = {}
                for tb in range(TB):
                    for hf in range(2):
                        bn = "A" if (tb * 2 + hf) % 2 == 0 else "B"
                        bk = bank[bn]
                        w0 = [ev_xn1, ev_lwq, pAB_rel[bn], ev_brow]
                        for k in range(KP):
                            sch.op("t", lambda e, tb=tb, hf=hf, k=k, bk=bk: e.matmul(
                                bk[:],
                                xn_b[:, k * NT + tb * 128: k * NT + tb * 128 + 128],
                                kq(k)[:, hf * 512:(hf + 1) * 512],
                                start=(k == 0), stop=False),
                                waits=w0 if k == 0 else ())
                        last = sch.op("t", lambda e, hf=hf, bk=bk: e.matmul(
                            bk[:], ones1b[:], xs[0:1, 25600 + hf * 512: 25600 + (hf + 1) * 512],
                            start=False, stop=True), inc="T")
                        ev_q[(tb, hf)] = sch.op("v", lambda e, tb=tb, hf=hf, bk=bk: e.tensor_relu(
                            qtok(tb)[:, hf * 512:(hf + 1) * 512], bk[:]),
                            inc="V",
                            waits=[last, free_xs2] if (tb == 0 and hf == 0) else [last])
                        pAB_rel[bn] = ev_q[(tb, hf)]
                ev_qT = None
                for tb in range(TB):
                    half = tb % 2
                    ev_tr = None
                    for p4 in range(2):
                        for pi in range(4):
                            p = p4 * 4 + pi
                            ev_tr = sch.op("t", lambda e, tb=tb, p=p, pi=pi, half=half: e.transpose(
                                pF[:, half * 512 + pi * 128: half * 512 + (pi + 1) * 128],
                                qtok(tb)[:, p * 128:(p + 1) * 128], idb[:]),
                                inc="T" if pi == 3 else None,
                                waits=[ev_q[(tb, p // 4)], ev_idb, pF_rel[0]]
                                if pi == 0 else ())
                        for pi in range(4):
                            p = p4 * 4 + pi
                            sch.op("v", lambda e, tb=tb, p=p, pi=pi, half=half: e.tensor_copy(
                                qTs(p)[:, tb * 128:(tb + 1) * 128],
                                pF[:, half * 512 + pi * 128: half * 512 + (pi + 1) * 128]),
                                inc=None, waits=[ev_tr] if pi == 0 else ())
                            sch.op("v", lambda e, tb=tb, p=p, pi=pi, half=half: e.tensor_copy(
                                qsh(2 * p)[:, tb * 128:(tb + 1) * 128],
                                pF[0:64, half * 512 + pi * 128: half * 512 + (pi + 1) * 128]))
                            ev_cp = sch.op("v", lambda e, tb=tb, p=p, pi=pi, half=half: e.tensor_copy(
                                qsh(2 * p + 1)[:, tb * 128:(tb + 1) * 128],
                                pF[64:128, half * 512 + pi * 128: half * 512 + (pi + 1) * 128]),
                                inc="V")
                        pF_rel[0] = ev_cp
                        ev_qT = ev_cp
                ev_den = None
                for tb in range(TB):
                    for k in range(KP):
                        ev_den = sch.op("t", lambda e, tb=tb, k=k: e.matmul(
                            pE[:, tb * 16: tb * 16 + 16],
                            qTs(k)[:, tb * 128:(tb + 1) * 128],
                            KS[:, k * 16:(k + 1) * 16],
                            start=(tb == 0 and k == 0), stop=(k == KP - 1),
                            skip_group_check=True),
                            inc="T" if (tb == TB - 1 and k == KP - 1) else None,
                            waits=[ev_qT, evK, rel["E"]] if (tb == 0 and k == 0) else ())
                ev_r = sch.op("v", lambda e: e.tensor_scalar_max(
                    rtok[:], pE[:, 0:TB * 16], EPS_DEN), inc="V", waits=[ev_den])
                sch.edge("v", "V")
                ev_r = sch.op("v", lambda e: e.reciprocal(rtok[:], rtok[:]), inc="V")
                sch.edge("v", "V")
                ev_rbf = sch.op("v", lambda e: e.tensor_copy(rtok_bf[:], rtok[:]), inc="V")
                sch.edge("v", "V")
                rel["E"] = ev_rbf
                ev_y = []
                htr = {}
                hbc = {}
                rbc_rel = [None, None]

                def stageA(h):
                    half = h % 2
                    ev_tr = None
                    for tb in range(TB):
                        ev_tr = sch.op("t", lambda e, tb=tb, half=half, h=h: e.transpose(
                            pF[0:1, half * 512 + tb * 128: half * 512 + tb * 128 + 128],
                            rtok_bf[:, tb * 16 + h: tb * 16 + h + 1], idb[:]),
                            inc="T" if tb == TB - 1 else None,
                            waits=[ev_rbf, pF_rel[0]] if tb == 0 else ())
                    htr[h] = ev_tr

                def stageB(h):
                    half = h % 2
                    ev_cr = sch.op("v", lambda e, half=half: e.tensor_copy(
                        crow[0:1, half * NT:(half + 1) * NT],
                        pF[0:1, half * 512:(half + 1) * 512]), inc="V",
                        waits=[htr[h], crow_rel[half]])
                    pF_rel[0] = ev_cr
                    ev_bc = sch.op("t", lambda e, half=half: e.matmul(
                        pH[0:64, :], ones1b[:, 0:64],
                        crow[0:1, half * NT:(half + 1) * NT], start=True, stop=True),
                        inc="T", waits=[ev_cr, rel["H"]])
                    crow_rel[half] = ev_bc
                    ev_rc = sch.op("v", lambda e, half=half: e.tensor_copy(
                        rbc(half), pH[0:64, :]), inc="V",
                        waits=[ev_bc, rbc_rel[half]])
                    rel["H"] = ev_rc
                    hbc[h] = ev_rc

                def stageC(h):
                    half = h % 2
                    bnc = "C" if half == 0 else "D"
                    bkc = bank[bnc]
                    ev_ctx = sch.op("t", lambda e, h=h, bkc=bkc: e.matmul(
                        bkc[0:64, :], wW1[0:64, h * HD:(h + 1) * HD], qsh(h),
                        start=True, stop=True),
                        inc="T", waits=[ev1, ev2, ev_qT, rel[bnc]])
                    p = h // 2
                    r0 = (h % 2) * 64
                    ev_m = sch.op("v", lambda e, p=p, r0=r0, bkc=bkc, half=half: e.tensor_tensor(
                        acc[r0:r0 + 64, p * NT:(p + 1) * NT], bkc[0:64, :],
                        rbc(half), OP.mult),
                        inc="V",
                        waits=[ev_ctx, hbc[h], free["acc"]] if h == 0 else [ev_ctx, hbc[h]])
                    rbc_rel[half] = ev_m
                    sch.edge("v", "V")
                    ev_a = sch.op("v", lambda e, p=p, r0=r0: e.tensor_add(
                        acc[r0:r0 + 64, p * NT:(p + 1) * NT],
                        acc[r0:r0 + 64, p * NT:(p + 1) * NT],
                        xn_b[r0:r0 + 64, p * NT:(p + 1) * NT]),
                        inc="V", waits=[ev_xn1])
                    ev_y.append(ev_a)
                    rel[bnc] = ev_a

                stageA(0)
                for h in range(NH):
                    stageB(h)
                    if h + 1 < NH:
                        stageA(h + 1)
                    stageC(h)
                ev_y_all = ev_y[-1]
                sch.wait("v", ev_y[-2])

                yk = lambda k: acc[:, k * NT:(k + 1) * NT]
                rel["A"] = pAB_rel["A"]
                rel["B"] = pAB_rel["B"]
                ev_ab, ev_pab = emit_ln(yk, ybf, ybf, ev_y_all, "A", "B", sq2,
                                        cast_bf=True, apply=False,
                                        wait_sq_free=prev_xn2_st)
                for k in range(KP):
                    sch.op("v", lambda e, k=k: e.tensor_mul(yk(k), yk(k), ln3[:]),
                           inc="V" if k == KP - 1 else None,
                           waits=[ev_ab] if k == 0 else ())
                sch.edge("v", "V")
                for k in range(KP):
                    sch.op("v", lambda e, k=k: e.tensor_add(yk(k), yk(k), ln1[:]),
                           inc="V" if k == KP - 1 else None)
                sch.edge("v", "V")
                for k in range(KP):
                    sch.op("v", lambda e, k=k: e.tensor_scalar(
                        yk(k), yk(k), gb_sb[:, k:k + 1], gb_sb[:, 8 + k:9 + k],
                        OP.mult, OP.add),
                        inc="V" if k == KP - 1 else None,
                        waits=[ev_gb] if k == 0 else ())
                sch.edge("v", "V")
                ev_x2 = None
                for k in range(KP):
                    ev_x2 = sch.op("v", lambda e, k=k: e.tensor_add(
                        yk(k), yk(k), x_res[:, k * NT:(k + 1) * NT]),
                        inc="V" if k == KP - 1 else None,
                        waits=[ev_x1] if k == 0 else ())
                sch.edge("v", "V")
                free["x_res"] = ev_x2
                free["xn"] = ev_y_all
                ev_stx2 = sch.dma("g", featmaj(x2T)[:, :, c0:c0 + NT],
                                  acc[:].rearrange("p (k n) -> p k n", n=NT), "ST",
                                  waits=[ev_x2])
                if PHASES == "f1a":
                    sch.dma("g", featmaj(outT)[:, :, c0:c0 + NT],
                            acc[:].rearrange("p (k n) -> p k n", n=NT), "ST",
                            waits=[ev_x2])
                st_events[("x2", ti)] = ev_stx2
                prev_x2_st = ev_stx2
                ev_xn2, ev_pab2 = emit_ln(yk, ybf, ybf, ev_x2, "A", "B", sq2,
                                          cast_bf=True, apply=True,
                                          wait_dst_free=prev_xn2_st)
                pAB_rel["A"] = ev_pab2
                pAB_rel["B"] = ev_pab2
                ev_stxn2 = sch.dma(
                    "g", featmaj(xn2T)[:, :, c0:c0 + NT],
                    xs[:, 12288:12288 + KP * NT].rearrange("p (k n) -> p k n", n=NT),
                    "ST", waits=[ev_xn2])
                st_events[("xn2", ti)] = ev_stxn2
                prev_xn2_st = ev_stxn2
                free["acc"] = [ev_stx2, ("S", sch.cnt["S"]), ("T", sch.cnt["T"])]
                free_xs2 = [("T", sch.cnt["T"]), ev_stxn2]

            free["xs"] = free_xs2

        if PHASES == "f1":
            emit_ffn(1, xT, xbT, outT, "x1")
        else:
            emit_ffn(1, xT, xbT, x1T, "x1")
            emit_attn()
        bar2 = sch.barrier()
        wW_rel[0] = bar2
        wW_rel[1] = bar2
        wD_rel[0] = bar2
        wD_rel[1] = bar2
        pAB_rel["A"] = bar2
        pAB_rel["B"] = bar2
        for b in rel:
            rel[b] = bar2
        sig_rel[0] = bar2
        sig_rel[1] = bar2
        if PHASES == "all":
            emit_ffn(2, x2T, xn2T, outT, "out")

        sch.op("s", lambda e: e.activation(eps_t[:, 0:1], eps_t[:, 0:1], AF.Copy),
               waits=[("ST", sch.cnt["ST"])])
        sch.op("g", lambda e: e.engine_nop(), waits=[("AL", sch.cnt["AL"])])

        for qn in sch.lastw:
            for s, v in sch.lastw[qn].items():
                assert v <= sch.cnt[s], (qn, s, v, sch.cnt[s])

        with nc.Block() as block:
            block.tensor(lambda e: [f(e) for f in sch.q["t"]])
            block.vector(lambda e: [f(e) for f in sch.q["v"]])
            block.scalar(lambda e: [f(e) for f in sch.q["s"]])
            block.gpsimd(lambda e: [f(e) for f in sch.q["g"]])
            block.sync(lambda e: [f(e) for f in sch.q["y"]])

    return nc


def prep_inputs(L0_attn_x, L0_attn_Wq, L0_attn_bq, L0_attn_Wk, L0_attn_bk,
                L0_attn_Wv, L0_attn_bv, L0_attn_gamma, L0_attn_beta,
                L0_ffn_router_gate, L0_ffn_w_gate, L0_ffn_router_up, L0_ffn_w_up,
                L0_ffn_router_down, L0_ffn_w_down):
    f = np.float32
    x = np.asarray(L0_attn_x, f)
    B = x.shape[0]

    def blockify_gu(w):
        wt = np.asarray(w, f).transpose(0, 2, 1).reshape(E, KP, 128, MG, 128)
        return np.ascontiguousarray(wt.transpose(3, 2, 0, 1, 4))

    def blockify_d(w):
        wt = np.asarray(w, f).transpose(0, 2, 1).reshape(E, KD, 128, MD, 128)
        return np.ascontiguousarray(wt.transpose(3, 2, 0, 1, 4))

    gamma = np.asarray(L0_attn_gamma, f).reshape(KP, 128).T
    beta = np.asarray(L0_attn_beta, f).reshape(KP, 128).T
    shared = {
        "wgF": blockify_gu(L0_ffn_w_gate),
        "wuF": blockify_gu(L0_ffn_w_up),
        "wdF": blockify_d(L0_ffn_w_down),
        "wqF": np.ascontiguousarray(np.asarray(L0_attn_Wq, f).T),
        "wkF": np.ascontiguousarray(np.asarray(L0_attn_Wk, f).T),
        "wvF": np.ascontiguousarray(np.asarray(L0_attn_Wv, f).T),
        "rguF": np.ascontiguousarray(np.concatenate(
            [np.asarray(L0_ffn_router_gate, f).T,
             np.asarray(L0_ffn_router_up, f).T], axis=1)),
        "rdF": np.ascontiguousarray(np.asarray(L0_ffn_router_down, f).T),
        "browF": np.concatenate(
            [np.asarray(L0_attn_bq, f), np.asarray(L0_attn_bk, f),
             np.asarray(L0_attn_bv, f)]).reshape(1, 3 * D),
        "gbF": np.ascontiguousarray(np.concatenate([gamma, beta], axis=1)),
        "idF": np.eye(128, dtype=f),
    }
    in_maps = []
    for b in range(B):
        m = dict(shared)
        m["xT"] = np.ascontiguousarray(x[b].T)
        in_maps.append(m)
    return in_maps


def kernel(**inputs):
    in_maps = prep_inputs(**inputs)
    nc = build_program(n_tiles=T_FULL // NT)
    res = run_bass_kernel_spmd(nc, in_maps, core_ids=list(range(len(in_maps))))
    outs = [np.ascontiguousarray(np.asarray(r["outT"], np.float32).T)
            for r in res.results]
    return np.stack(outs, axis=0)


# version 13 (final)
# speedup vs baseline: 1.1386x; 1.1292x over previous
import numpy as np
from contextlib import ExitStack
import concourse.bass as bass
import concourse.mybir as mybir
from concourse.bass_utils import run_bass_kernel_spmd

F32 = mybir.dt.float32
BF16 = mybir.dt.bfloat16
AF = mybir.ActivationFunctionType
OP = mybir.AluOpType
AX = mybir.AxisListType.X

D = 1024
FF = 4096
E = 8
NH = 16
HD = 64
KP = D // 128
MG = FF // 128
MD = KP
KD = FF // 128
NT = 512
TB = NT // 128
EPS_LN = 1e-5
EPS_DEN = 1e-6
BIG = 1.0e30
N_CORES = 8
T_FULL = 8192
DBG_TAKE = 30
PHASES = "all"


class Sch:
    def __init__(self, sems):
        self.sems = sems
        self.cnt = {k: 0 for k in sems}
        self.q = {k: [] for k in "tvsgy"}
        self.lastw = {qn: {} for qn in "tvsgy"}

    def wait(self, qn, ev):
        if ev is None:
            return
        if isinstance(ev, list):
            for e in ev:
                self.wait(qn, e)
            return
        s, v = ev
        if self.lastw[qn].get(s, -1) >= v:
            return
        self.lastw[qn][s] = v
        h = self.sems[s]
        self.q[qn].append(lambda e, h=h, v=v: e.wait_ge(h, v))

    def op(self, qn, fn, inc=None, waits=(), d=1):
        for ev in waits:
            self.wait(qn, ev)
        if inc is None:
            self.q[qn].append(fn)
            return None
        self.cnt[inc] += d
        ev = (inc, self.cnt[inc])
        h = self.sems[inc]
        self.q[qn].append(lambda e, fn=fn, h=h, d=d: fn(e).then_inc(h, d))
        return ev

    def dma(self, qn, out, in_, inc, waits=()):
        ev = self.op(qn, lambda e, o=out, i=in_: e.dma_start(out=o, in_=i),
                     inc=inc, waits=waits, d=16)
        return ev

    def edge(self, qn, name):
        self.wait(qn, (name, self.cnt[name]))

    def barrier(self):
        return [(n, self.cnt[n]) for n in ("T", "V", "S", "WL", "AL", "ST", "G")
                if self.cnt[n] > 0]


def build_program(n_tiles=16):
    T = NT * n_tiles
    nc = bass.Bass("TRN2", target_bir_lowering=False)
    dtens = lambda name, shape, dty, kind: nc.dram_tensor(name, shape, dty, kind=kind).ap()

    xT = dtens("xT", [D, T], F32, "ExternalInput")
    wgF = dtens("wgF", [MG, 128, E, KP, 128], F32, "ExternalInput")
    wuF = dtens("wuF", [MG, 128, E, KP, 128], F32, "ExternalInput")
    wdF = dtens("wdF", [MD, 128, E, KD, 128], F32, "ExternalInput")
    wqF = dtens("wqF", [D, D], F32, "ExternalInput")
    wkF = dtens("wkF", [D, D], F32, "ExternalInput")
    wvF = dtens("wvF", [D, D], F32, "ExternalInput")
    rguF = dtens("rguF", [D, 16], F32, "ExternalInput")
    rdF = dtens("rdF", [FF, 8], F32, "ExternalInput")
    browF = dtens("browF", [1, 3 * D], F32, "ExternalInput")
    gbF = dtens("gbF", [128, 16], F32, "ExternalInput")
    idF = dtens("idF", [128, 128], F32, "ExternalInput")
    outT = dtens("outT", [D, T], F32, "ExternalOutput")

    wgB = dtens("wgB", [MG, 128, E, KP, 128], BF16, "Internal")
    wuB = dtens("wuB", [MG, 128, E, KP, 128], BF16, "Internal")
    wdB = dtens("wdB", [MD, 128, E, KD, 128], BF16, "Internal")
    wqB = dtens("wqB", [D, D], BF16, "Internal")
    wkB = dtens("wkB", [D, D], BF16, "Internal")
    wvB = dtens("wvB", [D, D], BF16, "Internal")
    xbT = dtens("xbT", [D, T], BF16, "Internal")
    x1T = dtens("x1T", [D, T], F32, "Internal")
    xn1T = dtens("xn1T", [D, T], BF16, "Internal")
    x2T = dtens("x2T", [D, T], F32, "Internal")
    xn2T = dtens("xn2T", [D, T], BF16, "Internal")

    def featmaj(ap_):
        return ap_.rearrange("(k p) n -> p k n", p=128)

    with ExitStack() as ctx:
        en = ctx.enter_context
        x_res = en(nc.sbuf_tensor("x_res", [128, KP * NT], F32))
        xn_b = en(nc.sbuf_tensor("xn_b", [128, KP * NT], BF16))
        xs = en(nc.sbuf_tensor("xs", [128, E * KP * NT], BF16))
        t1 = en(nc.sbuf_tensor("t1", [128, MG * NT], BF16))
        acc = en(nc.sbuf_tensor("acc", [128, KP * NT], F32))
        wW0 = en(nc.sbuf_tensor("wW0", [128, E * KP * 128], BF16))
        wW1 = en(nc.sbuf_tensor("wW1", [128, E * KP * 128], BF16))
        wD0 = en(nc.sbuf_tensor("wD0", [128, KD * 128], BF16))
        wD1 = en(nc.sbuf_tensor("wD1", [128, KD * 128], BF16))
        bcd = en(nc.sbuf_tensor("bcd", [128, 2 * NT], BF16))
        bcg = en(nc.sbuf_tensor("bcg", [128, 2 * NT], BF16))
        sigt = en(nc.sbuf_tensor("sigt", [128, 2 * NT], BF16))
        ln0 = en(nc.sbuf_tensor("ln0", [128, NT], F32))
        ln1 = en(nc.sbuf_tensor("ln1", [128, NT], F32))
        ln2 = en(nc.sbuf_tensor("ln2", [128, NT], F32))
        tmpd = en(nc.sbuf_tensor("tmpd", [128, NT], F32))
        ln3 = en(nc.sbuf_tensor("ln3", [128, NT], F32))
        kv_sb = wW1[0:64, 0:NH * HD]
        ksf = en(nc.sbuf_tensor("ksf", [128, 8], F32))
        ksb = en(nc.sbuf_tensor("ksb", [128, 8], BF16))
        KS = en(nc.sbuf_tensor("KS", [128, KP * 16], BF16))
        rtok = en(nc.sbuf_tensor("rtok", [128, TB * 16], F32))
        rtok_bf = en(nc.sbuf_tensor("rtok_bf", [128, TB * 16], BF16))
        rgu_sb = en(nc.sbuf_tensor("rgu_sb", [128, KP * 16], BF16))
        rd_sb = en(nc.sbuf_tensor("rd_sb", [128, KD * 8], BF16))
        gb_sb = en(nc.sbuf_tensor("gb_sb", [128, 16], F32))
        idb = en(nc.sbuf_tensor("idb", [128, 128], BF16))
        idf = en(nc.sbuf_tensor("idf", [128, 128], F32))
        rt_sb = ln0[0:16, :]
        ones128 = en(nc.sbuf_tensor("ones128", [128, 128], BF16))
        ones1b = en(nc.sbuf_tensor("ones1b", [1, 128], BF16))
        onescol = en(nc.sbuf_tensor("onescol", [128, 1], BF16))
        eps_t = en(nc.sbuf_tensor("eps_t", [128, 1], F32))
        comb = en(nc.sbuf_tensor("comb", [128, TB * 16], F32))
        comb_bf = en(nc.sbuf_tensor("comb_bf", [128, TB * 16], BF16))
        combd = en(nc.sbuf_tensor("combd", [128, TB * 8], F32))
        combd_bf = en(nc.sbuf_tensor("combd_bf", [128, TB * 8], BF16))
        rm1 = en(nc.sbuf_tensor("rm1", [128, 12], F32))
        rm2 = en(nc.sbuf_tensor("rm2", [128, 12], F32))
        rp1 = en(nc.sbuf_tensor("rp1", [128, 12], F32))
        rp2 = en(nc.sbuf_tensor("rp2", [128, 12], F32))
        rmsk = en(nc.sbuf_tensor("rmsk", [128, TB * 24], F32))
        rl2 = en(nc.sbuf_tensor("rl2", [128, TB * 24], F32))
        crow = en(nc.sbuf_tensor("crow", [1, 2 * NT], BF16))
        brow_sb = xs[0:1, 25600:25600 + 3 * D]

        pA = en(nc.psum_tensor("pA", [128, 512], F32))
        pB = en(nc.psum_tensor("pB", [128, 512], F32))
        pC = en(nc.psum_tensor("pC", [128, 512], F32))
        pD = en(nc.psum_tensor("pD", [128, 512], F32))
        pE = en(nc.psum_tensor("pE", [128, 512], F32))
        pF = en(nc.psum_tensor("pF", [128, 1024], BF16))
        pG = en(nc.psum_tensor("pG", [128, 512], F32))
        pH = en(nc.psum_tensor("pH", [128, 512], F32))

        sems = {n: en(nc.semaphore(n)) for n in ("T", "V", "S", "G", "WL", "WL2", "WL3", "WL4", "AL", "AL2", "ST")}
        sch = Sch(sems)

        rel = {b: None for b in ("A", "B", "C", "D", "E", "F", "G", "H")}
        bank = {"A": pA, "B": pB, "C": pC, "D": pD, "E": pE, "F": pF, "G": pG, "H": pH}

        ev_ones = sch.op("g", lambda e: e.memset(ones128[:], 1.0), inc="G")
        ev_ones1b = sch.op("g", lambda e: e.memset(ones1b[:], 1.0), inc="G")
        ev_onescol = sch.op("g", lambda e: e.memset(onescol[:], 1.0), inc="G")
        ev_ksz = sch.op("g", lambda e: e.memset(KS[:], 0.0), inc="G")
        ev_eps = sch.op("g", lambda e: e.memset(eps_t[:], EPS_LN), inc="G")
        ev_gb = sch.dma("g", gb_sb[:], gbF, "AL")
        ev_idb = sch.dma("g", idb[:], idF, "AL")
        ev_idf = sch.dma("g", idf[:], idF, "AL")
        ev_rgu = sch.dma("g", rgu_sb[:].rearrange("p (k c) -> p k c", c=16),
                         rguF.rearrange("(k p) c -> p k c", p=128), "AL")
        ev_rd = sch.dma("g", rd_sb[:].rearrange("p (k c) -> p k c", c=8),
                        rdF.rearrange("(k p) c -> p k c", p=128), "AL")
        ev_wg = [sch.dma("g", wgB[c:c + 4], wgF[c:c + 4], "AL") for c in range(0, MG, 4)]
        ev_wu = [sch.dma("g", wuB[c:c + 4], wuF[c:c + 4], "AL") for c in range(0, MG, 4)]
        ev_wd = [sch.dma("g", wdB[c:c + 1], wdF[c:c + 1], "AL") for c in range(MD)]
        ev_xbt = sch.dma("g", xbT, xT, "AL")
        ev_wq = sch.dma("g", wqB, wqF, "AL")
        ev_wk = sch.dma("g", wkB, wkF, "AL")
        ev_wv = sch.dma("g", wvB, wvF, "AL")

        def emit_ln(src_f32, src_bf, dst_bf, wait_src, b1n, b2n, sq_of,
                    cast_bf=False, apply=True, wait_dst_free=None, wait_sq_free=None):
            b1, b2 = bank[b1n], bank[b2n]
            if cast_bf:
                ev_cast = None
                for k in range(KP):
                    ev_cast = sch.op("s", lambda e, k=k: e.activation(
                        src_bf(k), src_f32(k), AF.Copy), inc="S",
                        waits=([wait_src, wait_sq_free] if k == 0 else ()))
            else:
                ev_cast = wait_src
            ev_sq = None
            for k in range(KP):
                ev_sq = sch.op("s", lambda e, k=k: e.activation(
                    sq_of(k), src_f32(k), AF.Square), inc="S",
                    waits=([wait_src, wait_sq_free] if k == 0 else ()))
            ev_s1 = None
            for k in range(KP):
                ev_s1 = sch.op("t", lambda e, k=k, b1=b1: e.matmul(
                    b1[:], ones128[:], src_bf(k), start=(k == 0), stop=(k == KP - 1)),
                    inc="T" if k == KP - 1 else None,
                    waits=([ev_cast if cast_bf else wait_src, ev_ones, rel[b1n]]
                           if k == 0 else ()))
            ev_s2 = None
            for k in range(KP):
                ev_s2 = sch.op("t", lambda e, k=k, b2=b2: e.matmul(
                    b2[:], ones128[:], sq_of(k), start=(k == 0), stop=(k == KP - 1)),
                    inc="T" if k == KP - 1 else None,
                    waits=([ev_sq, rel[b2n]] if k == 0 else ()))
            c1 = 1.0 / D
            sch.op("v", lambda e: e.tensor_scalar_mul(ln0[:], b1[:], c1),
                   inc="V", waits=[ev_s1])
            ev_bfree = sch.op("v", lambda e: e.tensor_scalar_mul(ln1[:], b2[:], c1),
                              inc="V", waits=[ev_s2])
            rel[b1n] = ev_bfree
            rel[b2n] = ev_bfree
            sch.edge("v", "V")
            sch.op("v", lambda e: e.tensor_mul(ln2[:], ln0[:], ln0[:]), inc="V")
            sch.edge("v", "V")
            ev = sch.op("v", lambda e: e.tensor_sub(ln1[:], ln1[:], ln2[:]), inc="V")
            ev = sch.op("s", lambda e: e.activation(ln2[:], ln1[:], AF.Sqrt,
                                                    bias=eps_t[:, 0:1]),
                        inc="S", waits=[ev, ev_eps])
            ev = sch.op("v", lambda e: e.reciprocal(ln3[:], ln2[:]), inc="V", waits=[ev])
            sch.edge("v", "V")
            ev_ab = sch.op("v", lambda e: e.scalar_tensor_tensor(
                ln1[:], ln0[:], -1.0, ln3[:], OP.mult, OP.mult), inc="V")
            sch.edge("v", "V")
            if not apply:
                return ev_ab, ev_bfree
            for k in range(KP):
                sch.op("v", lambda e, k=k: e.tensor_mul(dst_bf(k), src_f32(k), ln3[:]),
                       inc="V" if k == KP - 1 else None,
                       waits=([wait_dst_free] if (k == 0 and wait_dst_free) else ()))
            sch.edge("v", "V")
            ev_ap = None
            for k in range(KP):
                ev_ap = sch.op("v", lambda e, k=k: e.tensor_add(dst_bf(k), dst_bf(k), ln1[:]),
                               inc="V" if k == KP - 1 else None)
            sch.edge("v", "V")
            return ev_ap, ev_bfree

        def emit_top2(psrc, n_r, combo, combo_bf, moff, mskoff, wait_logits):
            for j in range(n_r):
                sch.op("v", lambda e, j=j: e.reduce_max(
                    rm1[:, moff + j:moff + j + 1], psrc(j), axis=AX),
                    inc="V" if j == n_r - 1 else None,
                    waits=[wait_logits] if j == 0 else ())
            sch.edge("v", "V")
            for j in range(n_r):
                sch.op("v", lambda e, j=j: e.tensor_scalar(
                    rmsk[:, mskoff + j * 8: mskoff + (j + 1) * 8], psrc(j),
                    rm1[:, moff + j:moff + j + 1], None, OP.is_equal),
                    inc="V" if j == n_r - 1 else None)
            sch.edge("v", "V")
            for j in range(n_r):
                sch.op("v", lambda e, j=j: e.scalar_tensor_tensor(
                    rl2[:, mskoff + j * 8: mskoff + (j + 1) * 8],
                    rmsk[:, mskoff + j * 8: mskoff + (j + 1) * 8], -BIG,
                    psrc(j), OP.mult, OP.add),
                    inc="V" if j == n_r - 1 else None)
            sch.edge("v", "V")
            for j in range(n_r):
                sch.op("v", lambda e, j=j: e.reduce_max(
                    rm2[:, moff + j:moff + j + 1],
                    rl2[:, mskoff + j * 8: mskoff + (j + 1) * 8], axis=AX),
                    inc="V" if j == n_r - 1 else None)
            sch.edge("v", "V")
            ev = sch.op("v", lambda e: e.tensor_sub(
                rp1[:, moff:moff + n_r], rm1[:, moff:moff + n_r],
                rm2[:, moff:moff + n_r]), inc="V")
            ev = sch.op("s", lambda e: e.activation(
                rp1[:, moff:moff + n_r], rp1[:, moff:moff + n_r], AF.Sigmoid),
                inc="S", waits=[ev])
            ev = sch.op("v", lambda e: e.tensor_scalar(
                rp2[:, moff:moff + n_r], rp1[:, moff:moff + n_r], -1.0, 1.0,
                OP.mult, OP.add), inc="V", waits=[ev])
            sch.edge("v", "V")
            for j in range(n_r):
                sch.op("v", lambda e, j=j: e.tensor_scalar(
                    combo[:, j * 8:(j + 1) * 8],
                    rmsk[:, mskoff + j * 8: mskoff + (j + 1) * 8],
                    rp1[:, moff + j:moff + j + 1], None, OP.mult),
                    inc="V" if j == n_r - 1 else None)
            sch.edge("v", "V")
            for j in range(n_r):
                sch.op("v", lambda e, j=j: e.tensor_scalar(
                    rmsk[:, mskoff + j * 8: mskoff + (j + 1) * 8],
                    rl2[:, mskoff + j * 8: mskoff + (j + 1) * 8],
                    rm2[:, moff + j:moff + j + 1], None, OP.is_equal),
                    inc="V" if j == n_r - 1 else None)
            sch.edge("v", "V")
            for j in range(n_r):
                sch.op("v", lambda e, j=j: e.scalar_tensor_tensor(
                    combo[:, j * 8:(j + 1) * 8],
                    rmsk[:, mskoff + j * 8: mskoff + (j + 1) * 8],
                    rp2[:, moff + j:moff + j + 1],
                    combo[:, j * 8:(j + 1) * 8], OP.mult, OP.add),
                    inc="V" if j == n_r - 1 else None)
            sch.edge("v", "V")
            ev_bf = sch.op("v", lambda e: e.tensor_copy(combo_bf[:], combo[:]), inc="V")
            sch.edge("v", "V")
            return ev_bf

        pF_rel = [None]
        crow_rel = [None, None]
        bcg_rel = [None, None]
        bcd_rel = [None, None]

        def emit_bcast(col_of, slot, wait_src, dst_bf=None, dst_rel=None,
                       extra_tr_waits=()):
            half = slot
            ev_tr = None
            for tb in range(TB):
                ev_tr = sch.op("t", lambda e, tb=tb, half=half: e.transpose(
                    pF[0:1, half * 512 + tb * 128: half * 512 + tb * 128 + 128],
                    col_of(tb), idb[:]),
                    inc="T" if tb == TB - 1 else None,
                    waits=([wait_src, ev_idb, pF_rel[0]] + list(extra_tr_waits))
                    if tb == 0 else ())
            ev_cr = sch.op("v", lambda e, half=half: e.tensor_copy(
                crow[0:1, half * NT:(half + 1) * NT],
                pF[0:1, half * 512:(half + 1) * 512]), inc="V",
                waits=[ev_tr, crow_rel[half]])
            ev_bc = sch.op("t", lambda e: e.matmul(
                pH[:], ones1b[:],
                crow[0:1, half * NT:(half + 1) * NT],
                start=True, stop=True),
                inc="T", waits=[ev_cr, ev_ones1b, rel["H"]])
            crow_rel[half] = ev_bc
            pF_rel[0] = ev_cr
            ev = sch.op("s", lambda e, d=dst_bf: e.activation(
                d, pH[:], AF.Copy), inc="S", waits=[ev_bc, dst_rel])
            rel["H"] = ev
            return ev


        def bcast_steps(col_of, slot, wait_src_getter, dst_bf, dst_rel_getter,
                        done_cb):
            st = {}
            def s1():
                half = slot
                ev_tr = None
                for tb in range(TB):
                    ev_tr = sch.op("t", lambda e, tb=tb, half=half: e.transpose(
                        pF[0:1, half * 512 + tb * 128: half * 512 + tb * 128 + 128],
                        col_of(tb), idb[:]),
                        inc="T" if tb == TB - 1 else None,
                        waits=([wait_src_getter(), ev_idb, pF_rel[0]]) if tb == 0 else ())
                st["ev_tr"] = ev_tr
            def s2():
                half = slot
                ev_cr = sch.op("v", lambda e, half=half: e.tensor_copy(
                    crow[0:1, half * NT:(half + 1) * NT],
                    pF[0:1, half * 512:(half + 1) * 512]), inc="V",
                    waits=[st["ev_tr"], crow_rel[half]])
                pF_rel[0] = ev_cr
                st["ev_cr"] = ev_cr
            def s3():
                half = slot
                ev_bc = sch.op("t", lambda e, half=half: e.matmul(
                    pH[:], ones1b[:],
                    crow[0:1, half * NT:(half + 1) * NT],
                    start=True, stop=True),
                    inc="T", waits=[st["ev_cr"], ev_ones1b, rel["H"]])
                crow_rel[half] = ev_bc
                ev = sch.op("v", lambda e, d=dst_bf: e.tensor_copy(
                    d, pH[:]), inc="V", waits=[ev_bc, dst_rel_getter()])
                rel["H"] = ev
                done_cb(ev)
            return [s1, s2, s3]

        st_events = {}

        free = {"x_res": None, "xn": None, "xs": None, "acc": None}
        sig_rel = [None, None]
        pAB_rel = rel
        CYC4 = ("A", "B", "G")
        wW_rel = [None, None]
        wD_rel = [None, None]

        def emit_ffn(inv, srcT, src_bfT, dstT, dst_name):
            def make_head(ti):
                ctx = {}
                c0 = ti * NT

                def preA():
                    w_x = [free["x_res"]]
                    w_xn = [free["xn"]]
                    if inv == 2:
                        w_x.append(st_events.get(("x2", ti)))
                        w_xn.append(st_events.get(("xn2", ti)))
                    else:
                        w_xn.append(ev_xbt)
                    sch.dma(
                        "s", x_res[:].rearrange("p (k n) -> p k n", n=NT),
                        featmaj(srcT)[:, :, c0:c0 + NT], "AL2", waits=w_x)
                    ev_pair = sch.dma(
                        "s", xn_b[:].rearrange("p (k n) -> p k n", n=NT),
                        featmaj(src_bfT)[:, :, c0:c0 + NT], "AL2", waits=w_xn)
                    ctx["ev_x"] = ev_pair
                    ctx["ev_xn"] = ev_pair

                def preLN():
                    xr = lambda k: x_res[:, k * NT:(k + 1) * NT]
                    xnk = lambda k: xn_b[:, k * NT:(k + 1) * NT]
                    sqk = lambda k: xs[:, k * NT:(k + 1) * NT]
                    if inv == 1:
                        ev_ln, _ = emit_ln(xr, xnk, xnk, [ctx["ev_xn"], ctx["ev_x"]],
                                           "C", "D", sqk, wait_sq_free=free["xs"])
                        ctx["ev_sq_last"] = ("S", sch.cnt["S"])
                        ctx["ev_apply1"] = ("V", sch.cnt["V"])
                        ctx["ev_xn_ready"] = ev_ln
                    else:
                        ctx["ev_sq_last"] = None
                        ctx["ev_apply1"] = None
                        ctx["ev_xn_ready"] = ctx["ev_xn"]

                def preB():
                    ev_rt = None
                    for k in range(KP):
                        ev_rt = sch.op("t", lambda e, k=k: e.matmul(
                            pH[0:16, 0:NT], rgu_sb[:, k * 16:(k + 1) * 16],
                            xn_b[:, k * NT:(k + 1) * NT],
                            start=(k == 0), stop=(k == KP - 1),
                            skip_group_check=True),
                            inc="T" if k == KP - 1 else None,
                            waits=([ctx["ev_xn_ready"], ev_rgu, rel["H"]]
                                   if k == 0 else ()))
                    ev_rte = sch.op("v", lambda e: e.tensor_copy(
                        rt_sb[:], pH[0:16, 0:NT]), inc="V", waits=[ev_rt])
                    rel["H"] = ev_rte
                    ev_rtt = None
                    for tb in range(TB):
                        ev_rtt = sch.op("t", lambda e, tb=tb: e.transpose(
                            pE[:, tb * 16:(tb + 1) * 16],
                            rt_sb[0:16, tb * 128:(tb + 1) * 128], idf[0:16, 0:16]),
                            inc="T" if tb == TB - 1 else None,
                            waits=[ev_rte, ev_idf, rel["E"]] if tb == 0 else ())
                    ctx["ev_comb"] = emit_top2(
                        lambda j: pE[:, (j // 2) * 16 + (j % 2) * 8:
                                     (j // 2) * 16 + (j % 2) * 8 + 8],
                        TB * 2, comb, comb_bf, 0, 0, ev_rtt)

                def xs_steps(ei):
                    slot = ei % 2
                    hstate = {}
                    steps = bcast_steps(
                        lambda tb, ei=ei: comb_bf[:, tb * 16 + ei:tb * 16 + ei + 1],
                        slot, lambda: ctx["ev_comb"],
                        bcg[:, slot * NT:(slot + 1) * NT],
                        lambda slot=slot: bcg_rel[slot],
                        lambda ev: hstate.update(ev_cp=ev))
                    def tts():
                        last = None
                        for k in range(KP):
                            last = sch.op("v", lambda e, ei=ei, k=k, slot=slot: e.tensor_tensor(
                                xs[:, (ei * KP + k) * NT:(ei * KP + k + 1) * NT],
                                xn_b[:, k * NT:(k + 1) * NT],
                                bcg[:, slot * NT:(slot + 1) * NT], OP.mult),
                                inc="V" if k == KP - 1 else None,
                                waits=[hstate["ev_cp"], ctx["ev_xn_ready"], free["xs"]]
                                if k == 0 else ())
                        bcg_rel[slot] = last
                        ctx["ev_xs_gate"] = last
                    return steps + [tts]
                hsteps = [preA, preLN, preB]
                for ei in range(E):
                    hsteps.extend(xs_steps(ei))
                return ctx, hsteps

            ctx0, thunks0 = make_head(0)
            for th in thunks0:
                th()
            heads = {0: ctx0}

            for ti in range(n_tiles):
                c0 = ti * NT
                ctx = heads.pop(ti)
                ev_init = None
                for k in range(KP):
                    ev_init = sch.op("s", lambda e, k=k: e.activation(
                        acc[:, k * NT:(k + 1) * NT], x_res[:, k * NT:(k + 1) * NT],
                        AF.Copy, scale=2.0), inc="S" if k == KP - 1 else None,
                        waits=[free["acc"], ctx["ev_x"]] if k == 0 else ())
                free["x_res"] = [ev for ev in (ev_init, ctx["ev_sq_last"],
                                               ctx["ev_apply1"]) if ev]

                ev_gate = [None] * MG
                for m in range(MG):
                    buf = wW0 if m % 2 == 0 else wW1
                    bn = CYC4[m % 3]
                    bk = bank[bn]
                    ev_ld = sch.dma("y", buf[:].rearrange("p (e k c) -> p e k c", k=KP, c=128),
                                    wgB[m], "WL",
                                    waits=[ev_wg[m // 4], wW_rel[m % 2]])
                    w0 = [ev_ld, pAB_rel[bn]] + ([ctx["ev_xs_gate"]] if m == 0 else [])
                    last = None
                    for ei in range(E):
                        for k in range(KP):
                            last = sch.op("t", lambda e, ei=ei, k=k, buf=buf, bk=bk: e.matmul(
                                bk[:], buf[:, (ei * KP + k) * 128:(ei * KP + k + 1) * 128],
                                xs[:, (ei * KP + k) * NT:(ei * KP + k + 1) * NT],
                                start=(ei == 0 and k == 0),
                                stop=(ei == E - 1 and k == KP - 1)),
                                inc="T" if (ei == E - 1 and k == KP - 1) else None,
                                waits=w0 if (ei == 0 and k == 0) else ())
                    ev_gate[m] = last
                    wW_rel[m % 2] = last
                    ev_sg = sch.op("s", lambda e, m=m, bk=bk: e.activation(
                        sigt[:, (m % 2) * NT:(m % 2 + 1) * NT], bk[:], AF.Sigmoid),
                        inc="S", waits=[last, sig_rel[m % 2]])
                    ev_t1 = sch.op("v", lambda e, m=m, bk=bk: e.tensor_tensor(
                        t1[:, m * NT:(m + 1) * NT],
                        sigt[:, (m % 2) * NT:(m % 2 + 1) * NT], bk[:], OP.mult),
                        inc="V", waits=[ev_sg])
                    sig_rel[m % 2] = ev_t1
                    pAB_rel[bn] = ev_t1

                ev_xs_up = None
                for ei in range(E):
                    slot = ei % 2
                    ev_cp = emit_bcast(
                        lambda tb, ei=ei: comb_bf[:, tb * 16 + 8 + ei:tb * 16 + 8 + ei + 1],
                        slot, ctx["ev_comb"], dst_bf=bcg[:, slot * NT:(slot + 1) * NT],
                        dst_rel=bcg_rel[slot])
                    for k in range(KP):
                        ev_xs_up = sch.op("v", lambda e, ei=ei, k=k, slot=slot: e.tensor_tensor(
                            xs[:, (ei * KP + k) * NT:(ei * KP + k + 1) * NT],
                            xn_b[:, k * NT:(k + 1) * NT],
                            bcg[:, slot * NT:(slot + 1) * NT], OP.mult),
                            inc="V" if k == KP - 1 else None,
                            waits=[ev_cp, ev_gate[MG - 1]] if k == 0 else ())
                    bcg_rel[slot] = ev_xs_up
                free["xn"] = ev_xs_up

                if ti + 1 < n_tiles:
                    nctx, nthunks = make_head(ti + 1)
                    heads[ti + 1] = nctx
                    nthunks[0]()
                    nthunks = nthunks[1:]
                else:
                    nthunks = []

                ev_up = [None] * MG
                ev_drt = None
                drt_pend = []
                for m in range(MG):
                    buf = wW0 if m % 2 == 0 else wW1
                    bn = CYC4[m % 3]
                    bk = bank[bn]
                    ev_ld = sch.dma("y", buf[:].rearrange("p (e k c) -> p e k c", k=KP, c=128),
                                    wuB[m], "WL", waits=[ev_wu[m // 4], wW_rel[m % 2]])
                    w0 = [ev_ld, pAB_rel[bn]] + ([ev_xs_up] if m == 0 else [])
                    last = None
                    for ei in range(E):
                        for k in range(KP):
                            last = sch.op("t", lambda e, ei=ei, k=k, buf=buf, bk=bk: e.matmul(
                                bk[:], buf[:, (ei * KP + k) * 128:(ei * KP + k + 1) * 128],
                                xs[:, (ei * KP + k) * NT:(ei * KP + k + 1) * NT],
                                start=(ei == 0 and k == 0),
                                stop=(ei == E - 1 and k == KP - 1)),
                                inc="T" if (ei == E - 1 and k == KP - 1) else None,
                                waits=w0 if (ei == 0 and k == 0) else ())
                    ev_up[m] = last
                    wW_rel[m % 2] = last
                    ev_int = sch.op("v", lambda e, m=m, bk=bk: e.tensor_tensor(
                        t1[:, m * NT:(m + 1) * NT],
                        t1[:, m * NT:(m + 1) * NT], bk[:], OP.mult),
                        inc="V", waits=[last])
                    pAB_rel[bn] = ev_int
                    drt_pend.append((m, ev_int))
                    if len(drt_pend) > 1 or m == MG - 1:
                        todo = drt_pend if m == MG - 1 else [drt_pend.pop(0)]
                        for dm, dev in todo:
                            for tb in range(TB):
                                ev_drt = sch.op("t", lambda e, m=dm, tb=tb: e.matmul(
                                    pE[:, 64 + tb * 8: 64 + tb * 8 + 8],
                                    t1[:, m * NT + tb * 128: m * NT + tb * 128 + 128],
                                    rd_sb[:, m * 8:(m + 1) * 8],
                                    start=False, stop=(dm == MG - 1),
                                    skip_group_check=True),
                                    inc="T" if (dm == MG - 1 and tb == TB - 1) else None,
                                    waits=([dev, ev_rd] +
                                           ([ctx["ev_comb"]] if dm == 0 else [])) if tb == 0 else ())
                        if m == MG - 1:
                            drt_pend = []
                free["xs"] = ev_up[MG - 1]

                ev_comb_d = emit_top2(
                    lambda j: pE[:, 64 + j * 8: 64 + j * 8 + 8],
                    TB, combd, combd_bf, 8, TB * 16, ev_drt)
                rel["E"] = ev_comb_d

                micro = []
                sch._hd_taken = 0

                ev_res = None
                bcd_ev = {}
                def bcd_steps_for(ei):
                    slot = ei % 2
                    return bcast_steps(
                        lambda tb, ei=ei: combd_bf[:, tb * 8 + ei:tb * 8 + ei + 1],
                        slot, lambda: ev_comb_d,
                        bcd[:, slot * NT:(slot + 1) * NT],
                        lambda slot=slot: bcd_rel[slot],
                        lambda ev, ei=ei: bcd_ev.__setitem__(ei, ev))
                for stp in bcd_steps_for(0):
                    stp()
                for ei in range(E):
                    slot = ei % 2
                    if ei + 1 < E:
                        micro.extend(bcd_steps_for(ei + 1))
                    cap = DBG_TAKE if isinstance(DBG_TAKE, int) else (99 if DBG_TAKE else 0)
                    already = getattr(sch, "_hd_taken", 0)
                    take = min(5, len(nthunks), max(0, cap - already))
                    sch._hd_taken = already + take
                    micro.extend(nthunks[:take])
                    nthunks = nthunks[take:]
                    ev_bcd = bcd_ev[ei]
                    for m2 in range(MD):
                        bslot = (ei * MD + m2) % 2
                        buf = wD0 if bslot == 0 else wD1
                        bn = CYC4[(ei * MD + m2) % 3]
                        bk = bank[bn]
                        ev_ld = sch.dma("y", buf[:].rearrange("p (k c) -> p k c", c=128),
                                        wdB[m2, :, ei], "WL",
                                        waits=[ev_wd[m2], wD_rel[bslot]])
                        w0 = [ev_ld, pAB_rel[bn], ev_bcd]
                        if ei == 0 and m2 == 0:
                            w0.append(ev_init)
                        last = None
                        for k in range(KD):
                            last = sch.op("t", lambda e, k=k, buf=buf, bk=bk: e.matmul(
                                bk[:], buf[:, k * 128:(k + 1) * 128],
                                t1[:, k * NT:(k + 1) * NT],
                                start=(k == 0), stop=(k == KD - 1)),
                                inc="T" if k == KD - 1 else None,
                                waits=w0 if k == 0 else ())
                        wD_rel[bslot] = last
                        ev_t = sch.op("v", lambda e, bk=bk, slot=slot: e.tensor_tensor(
                            tmpd[:], bk[:], bcd[:, slot * NT:(slot + 1) * NT],
                            OP.mult), inc="V", waits=[last])
                        sch.edge("v", "V")
                        ev_ev = sch.op("v", lambda e, m2=m2: e.tensor_add(
                            acc[:, m2 * NT:(m2 + 1) * NT],
                            acc[:, m2 * NT:(m2 + 1) * NT], tmpd[:]), inc="V")
                        pAB_rel[bn] = ev_ev
                        bcd_rel[slot] = ev_t
                        if ei == E - 1:
                            sch.edge("v", "V")
                            ev_res = sch.op("v", lambda e, m2=m2: e.tensor_scalar_mul(
                                acc[:, m2 * NT:(m2 + 1) * NT],
                                acc[:, m2 * NT:(m2 + 1) * NT], 0.5), inc="V")
                        if micro:
                            micro.pop(0)()
                for th in micro:
                    th()
                for th in nthunks:
                    th()
                ev_st = sch.dma("s", featmaj(dstT)[:, :, c0:c0 + NT],
                                acc[:].rearrange("p (k n) -> p k n", n=NT), "ST",
                                waits=[ev_res])
                st_events[(dst_name, ti)] = ev_st
                free["acc"] = ev_st

        def emit_attn():
            bar = sch.barrier()
            ev_lwq = sch.dma("g", t1[:, 0:KP * 1024].rearrange("p (k c) -> p k c", c=1024),
                             featmaj(wqB), "AL", waits=[ev_wq] + bar)
            ev_lwk = sch.dma("g", t1[:, KP * 1024:2 * KP * 1024].rearrange(
                "p (k c) -> p k c", c=1024), featmaj(wkB), "AL", waits=[ev_wk])
            ev_lwv = sch.dma("g", wW0[:].rearrange("p (k c) -> p k c", c=1024),
                             featmaj(wvB), "AL", waits=[ev_wv])
            ev_brow = sch.dma("g", brow_sb, browF, "AL")
            kq = lambda k: t1[:, k * 1024:(k + 1) * 1024]
            kk = lambda k: t1[:, (KP + k) * 1024:(KP + k + 1) * 1024]
            kvw = lambda k: wW0[:, k * 1024:(k + 1) * 1024]
            ktok = lambda tb: xs[:, tb * 1024:(tb + 1) * 1024]
            vtok = lambda tb: xs[:, (TB + tb) * 1024:(TB + tb + 1) * 1024]
            sq_at = lambda k: xs[:, 8 * 1024 + k * NT: 8 * 1024 + (k + 1) * NT]

            def xnp(ti):
                return xn_b if ti % 2 == 0 else xs[:, 12288:16384]

            p1_free_buf = {0: free["xn"], 1: free["xs"]}
            p1_free_x = [free["x_res"]]
            p1_sq_free = [free["xs"]]
            p1ctx = {}

            def p1_head(ti):
                c0 = ti * NT
                buf = xnp(ti)
                ev_x = sch.dma("g", x_res[:].rearrange("p (k n) -> p k n", n=NT),
                               featmaj(x1T)[:, :, c0:c0 + NT], "AL",
                               waits=[p1_free_x[0], st_events[("x1", ti)]])
                ev_xr = sch.dma("g", buf.rearrange("p (k n) -> p k n", n=NT),
                                featmaj(x1T)[:, :, c0:c0 + NT], "AL",
                                waits=[p1_free_buf[ti % 2], st_events[("x1", ti)]])
                xr = lambda k: x_res[:, k * NT:(k + 1) * NT]
                xnk = lambda k, buf=buf: buf[:, k * NT:(k + 1) * NT]
                ev_ln, _ = emit_ln(xr, xnk, xnk, [ev_xr, ev_x], "G", "H", sq_at,
                                   wait_sq_free=p1_sq_free[0])
                p1_sq_free[0] = ("T", sch.cnt["T"])
                p1_free_x[0] = [ev_ln, ("S", sch.cnt["S"])]
                ev_stxn = sch.dma("g", featmaj(xn1T)[:, :, c0:c0 + NT],
                                  buf.rearrange("p (k n) -> p k n", n=NT), "ST",
                                  waits=[ev_ln])
                st_events[("xn1", ti)] = ev_stxn
                if PHASES == "p1":
                    sch.dma("g", featmaj(outT)[:, :, c0:c0 + NT],
                            buf.rearrange("p (k n) -> p k n", n=NT), "ST",
                            waits=[ev_ln])
                p1ctx[ti] = ev_ln

            def p1_proj(ti, which):
                buf = xnp(ti)
                ev_ln = p1ctx[ti]
                wt, boff, ev_lw = (kk, D, ev_lwk) if which == "k" else (kvw, 2 * D, ev_lwv)
                out = {}
                for tb in range(TB):
                    for hf in range(2):
                        bn = "A" if (tb * 2 + hf) % 2 == 0 else "B"
                        bk = bank[bn]
                        w0 = [ev_ln, ev_lw, pAB_rel[bn], ev_brow, ev_ones1b]
                        for k in range(KP):
                            sch.op("t", lambda e, tb=tb, hf=hf, k=k, bk=bk, wt=wt, buf=buf: e.matmul(
                                bk[:],
                                buf[:, k * NT + tb * 128: k * NT + tb * 128 + 128],
                                wt(k)[:, hf * 512:(hf + 1) * 512],
                                start=(k == 0), stop=False),
                                waits=w0 if k == 0 else ())
                        last = sch.op("t", lambda e, hf=hf, bk=bk, boff=boff: e.matmul(
                            bk[:], ones1b[:],
                            xs[0:1, 25600 + boff + hf * 512: 25600 + boff + (hf + 1) * 512],
                            start=False, stop=True), inc="T")
                        dst = (ktok if which == "k" else vtok)(tb)
                        if which == "k":
                            ev_e = sch.op("v", lambda e, dst=dst, hf=hf, bk=bk: e.tensor_relu(
                                dst[:, hf * 512:(hf + 1) * 512], bk[:]),
                                inc="V", waits=[last])
                        else:
                            ev_e = sch.op("v", lambda e, dst=dst, hf=hf, bk=bk: e.tensor_copy(
                                dst[:, hf * 512:(hf + 1) * 512], bk[:]),
                                inc="V", waits=[last])
                        pAB_rel[bn] = ev_e
                        out[tb] = ev_e
                return out

            p1_head(0)
            for ti in range(n_tiles):
                evk_k = p1_proj(ti, "k")
                if ti + 1 < n_tiles:
                    p1_head(ti + 1)
                evk_v = p1_proj(ti, "v")
                p1_free_buf[ti % 2] = [("T", sch.cnt["T"]), st_events[("xn1", ti)]]
                for tb in range(TB):
                    w_kv = [evk_k[tb], evk_v[tb]]
                    if ti == 0 and tb == 0:
                        w_kv += [rel["C"], rel["D"]]
                    for h in range(NH):
                        bn = "C" if h < 8 else "D"
                        bk = bank[bn]
                        hh = h % 8
                        st = (ti == 0 and tb == 0 and hh == 0)
                        ev_kv = sch.op("t", lambda e, tb=tb, h=h, hh=hh, bk=bk, st=st: e.matmul(
                            bk[0:64, hh * 64:(hh + 1) * 64],
                            ktok(tb)[:, h * HD:(h + 1) * HD],
                            vtok(tb)[:, h * HD:(h + 1) * HD],
                            start=st, stop=(ti == n_tiles - 1 and tb == TB - 1),
                            skip_group_check=True),
                            inc="T" if (ti == n_tiles - 1 and tb == TB - 1 and h == NH - 1) else None,
                            waits=w_kv if h == 0 else ())
                    for mm in range(KP):
                        st = (ti == 0 and tb == 0 and mm == 0)
                        ev_ks = sch.op("t", lambda e, tb=tb, mm=mm, st=st: e.matmul(
                            pE[:, mm:mm + 1],
                            ktok(tb)[:, mm * 128:(mm + 1) * 128], onescol[:],
                            start=st, stop=(ti == n_tiles - 1 and tb == TB - 1),
                            skip_group_check=True),
                            inc="T" if (ti == n_tiles - 1 and tb == TB - 1 and mm == KP - 1) else None,
                            waits=([ev_onescol, rel["E"]] if st else ()))
            free["xn"] = p1_free_buf[0] if (n_tiles - 1) % 2 == 1 else p1_free_buf[1]
            free["x_res"] = p1_free_x[0]

            ev_kv_done = ("T", sch.cnt["T"])
            ev1 = sch.op("v", lambda e: e.tensor_copy(wW1[0:64, 0:512], pC[0:64, :]),
                         inc="V", waits=[ev_kv_done])
            ev2 = sch.op("v", lambda e: e.tensor_copy(wW1[0:64, 512:1024], pD[0:64, :]),
                         inc="V")
            ev3 = sch.op("v", lambda e: e.tensor_copy(ksf[:], pE[:, 0:8]), inc="V")
            sch.edge("v", "V")
            ev4 = sch.op("v", lambda e: e.tensor_copy(ksb[:], ksf[:]), inc="V")
            sch.edge("v", "V")
            evK = None
            for k in range(KP):
                evK = sch.op("v", lambda e, k=k: e.tensor_copy(
                    KS[0:64, k * 16 + 2 * k: k * 16 + 2 * k + 1], ksb[0:64, k:k + 1]),
                    inc="V" if k == KP - 1 else None,
                    waits=[ev_ksz] if k == 0 else ())
            for k in range(KP):
                evK = sch.op("v", lambda e, k=k: e.tensor_copy(
                    KS[64:128, k * 16 + 2 * k + 1: k * 16 + 2 * k + 2],
                    ksb[64:128, k:k + 1]),
                    inc="V" if k == KP - 1 else None)
            sch.edge("v", "V")
            rel["C"] = ev1
            rel["D"] = ev2
            rel["E"] = ev3
            if PHASES == "p1":
                return

            qtok = lambda tb: xs[:, tb * 1024:(tb + 1) * 1024]
            qTs = lambda p: xs[:, 4096 + p * NT: 4096 + (p + 1) * NT]
            sq2 = lambda k: xs[:, 8192 + k * NT: 8192 + (k + 1) * NT]
            ybf = lambda k: xs[:, 12288 + k * NT: 12288 + (k + 1) * NT]
            qsh = lambda h: xs[0:64, 16384 + h * NT: 16384 + (h + 1) * NT]
            rbc = lambda sl: xs[0:64, 16384 + (16 + sl) * NT: 16384 + (17 + sl) * NT]
            free_xs2 = ("T", sch.cnt["T"])
            prev_xn2_st = None
            prev_x2_st = None

            for ti in range(n_tiles):
                c0 = ti * NT
                sch.dma("g", xn_b[:].rearrange("p (k n) -> p k n", n=NT),
                                featmaj(xn1T)[:, :, c0:c0 + NT], "AL",
                                waits=[free["xn"], st_events[("xn1", ti)]])
                ev_x1 = sch.dma("g", x_res[:].rearrange("p (k n) -> p k n", n=NT),
                                featmaj(x1T)[:, :, c0:c0 + NT], "AL",
                                waits=[free["x_res"]])
                ev_xn1 = ev_x1
                ev_q = {}
                for tb in range(TB):
                    for hf in range(2):
                        bn = "A" if (tb * 2 + hf) % 2 == 0 else "B"
                        bk = bank[bn]
                        w0 = [ev_xn1, ev_lwq, pAB_rel[bn], ev_brow]
                        for k in range(KP):
                            sch.op("t", lambda e, tb=tb, hf=hf, k=k, bk=bk: e.matmul(
                                bk[:],
                                xn_b[:, k * NT + tb * 128: k * NT + tb * 128 + 128],
                                kq(k)[:, hf * 512:(hf + 1) * 512],
                                start=(k == 0), stop=False),
                                waits=w0 if k == 0 else ())
                        last = sch.op("t", lambda e, hf=hf, bk=bk: e.matmul(
                            bk[:], ones1b[:], xs[0:1, 25600 + hf * 512: 25600 + (hf + 1) * 512],
                            start=False, stop=True), inc="T")
                        ev_q[(tb, hf)] = sch.op("v", lambda e, tb=tb, hf=hf, bk=bk: e.tensor_relu(
                            qtok(tb)[:, hf * 512:(hf + 1) * 512], bk[:]),
                            inc="V",
                            waits=[last, free_xs2] if (tb == 0 and hf == 0) else [last])
                        pAB_rel[bn] = ev_q[(tb, hf)]
                ev_qT = None
                for tb in range(TB):
                    half = tb % 2
                    ev_tr = None
                    for p4 in range(2):
                        for pi in range(4):
                            p = p4 * 4 + pi
                            ev_tr = sch.op("t", lambda e, tb=tb, p=p, pi=pi, half=half: e.transpose(
                                pF[:, half * 512 + pi * 128: half * 512 + (pi + 1) * 128],
                                qtok(tb)[:, p * 128:(p + 1) * 128], idb[:]),
                                inc="T" if pi == 3 else None,
                                waits=[ev_q[(tb, p // 4)], ev_idb, pF_rel[0]]
                                if pi == 0 else ())
                        for pi in range(4):
                            p = p4 * 4 + pi
                            sch.op("v", lambda e, tb=tb, p=p, pi=pi, half=half: e.tensor_copy(
                                qTs(p)[:, tb * 128:(tb + 1) * 128],
                                pF[:, half * 512 + pi * 128: half * 512 + (pi + 1) * 128]),
                                inc=None, waits=[ev_tr] if pi == 0 else ())
                            sch.op("v", lambda e, tb=tb, p=p, pi=pi, half=half: e.tensor_copy(
                                qsh(2 * p)[:, tb * 128:(tb + 1) * 128],
                                pF[0:64, half * 512 + pi * 128: half * 512 + (pi + 1) * 128]))
                            ev_cp = sch.op("v", lambda e, tb=tb, p=p, pi=pi, half=half: e.tensor_copy(
                                qsh(2 * p + 1)[:, tb * 128:(tb + 1) * 128],
                                pF[64:128, half * 512 + pi * 128: half * 512 + (pi + 1) * 128]),
                                inc="V")
                        pF_rel[0] = ev_cp
                        ev_qT = ev_cp
                ev_den = None
                for tb in range(TB):
                    for k in range(KP):
                        ev_den = sch.op("t", lambda e, tb=tb, k=k: e.matmul(
                            pE[:, tb * 16: tb * 16 + 16],
                            qTs(k)[:, tb * 128:(tb + 1) * 128],
                            KS[:, k * 16:(k + 1) * 16],
                            start=(tb == 0 and k == 0), stop=(k == KP - 1),
                            skip_group_check=True),
                            inc="T" if (tb == TB - 1 and k == KP - 1) else None,
                            waits=[ev_qT, evK, rel["E"]] if (tb == 0 and k == 0) else ())
                ev_r = sch.op("v", lambda e: e.tensor_scalar_max(
                    rtok[:], pE[:, 0:TB * 16], EPS_DEN), inc="V", waits=[ev_den])
                sch.edge("v", "V")
                ev_r = sch.op("v", lambda e: e.reciprocal(rtok[:], rtok[:]), inc="V")
                sch.edge("v", "V")
                ev_rbf = sch.op("v", lambda e: e.tensor_copy(rtok_bf[:], rtok[:]), inc="V")
                sch.edge("v", "V")
                rel["E"] = ev_rbf
                ev_y = []
                htr = {}
                hbc = {}
                rbc_rel = [None, None]

                def stageA(h):
                    half = h % 2
                    ev_tr = None
                    for tb in range(TB):
                        ev_tr = sch.op("t", lambda e, tb=tb, half=half, h=h: e.transpose(
                            pF[0:1, half * 512 + tb * 128: half * 512 + tb * 128 + 128],
                            rtok_bf[:, tb * 16 + h: tb * 16 + h + 1], idb[:]),
                            inc="T" if tb == TB - 1 else None,
                            waits=[ev_rbf, pF_rel[0]] if tb == 0 else ())
                    htr[h] = ev_tr

                def stageB(h):
                    half = h % 2
                    ev_cr = sch.op("v", lambda e, half=half: e.tensor_copy(
                        crow[0:1, half * NT:(half + 1) * NT],
                        pF[0:1, half * 512:(half + 1) * 512]), inc="V",
                        waits=[htr[h], crow_rel[half]])
                    pF_rel[0] = ev_cr
                    ev_bc = sch.op("t", lambda e, half=half: e.matmul(
                        pH[0:64, :], ones1b[:, 0:64],
                        crow[0:1, half * NT:(half + 1) * NT], start=True, stop=True),
                        inc="T", waits=[ev_cr, rel["H"]])
                    crow_rel[half] = ev_bc
                    ev_rc = sch.op("v", lambda e, half=half: e.tensor_copy(
                        rbc(half), pH[0:64, :]), inc="V",
                        waits=[ev_bc, rbc_rel[half]])
                    rel["H"] = ev_rc
                    hbc[h] = ev_rc

                def stageC(h):
                    half = h % 2
                    bnc = "C" if half == 0 else "D"
                    bkc = bank[bnc]
                    ev_ctx = sch.op("t", lambda e, h=h, bkc=bkc: e.matmul(
                        bkc[0:64, :], wW1[0:64, h * HD:(h + 1) * HD], qsh(h),
                        start=True, stop=True),
                        inc="T", waits=[ev1, ev2, ev_qT, rel[bnc]])
                    p = h // 2
                    r0 = (h % 2) * 64
                    ev_m = sch.op("v", lambda e, p=p, r0=r0, bkc=bkc, half=half: e.tensor_tensor(
                        acc[r0:r0 + 64, p * NT:(p + 1) * NT], bkc[0:64, :],
                        rbc(half), OP.mult),
                        inc="V",
                        waits=[ev_ctx, hbc[h], free["acc"]] if h == 0 else [ev_ctx, hbc[h]])
                    rbc_rel[half] = ev_m
                    sch.edge("v", "V")
                    ev_a = sch.op("v", lambda e, p=p, r0=r0: e.tensor_add(
                        acc[r0:r0 + 64, p * NT:(p + 1) * NT],
                        acc[r0:r0 + 64, p * NT:(p + 1) * NT],
                        xn_b[r0:r0 + 64, p * NT:(p + 1) * NT]),
                        inc="V", waits=[ev_xn1])
                    ev_y.append(ev_a)
                    rel[bnc] = ev_a

                stageA(0)
                for h in range(NH):
                    stageB(h)
                    if h + 1 < NH:
                        stageA(h + 1)
                    stageC(h)
                ev_y_all = ev_y[-1]
                sch.wait("v", ev_y[-2])

                yk = lambda k: acc[:, k * NT:(k + 1) * NT]
                rel["A"] = pAB_rel["A"]
                rel["B"] = pAB_rel["B"]
                ev_ab, ev_pab = emit_ln(yk, ybf, ybf, ev_y_all, "A", "B", sq2,
                                        cast_bf=True, apply=False,
                                        wait_sq_free=prev_xn2_st)
                for k in range(KP):
                    sch.op("v", lambda e, k=k: e.tensor_mul(yk(k), yk(k), ln3[:]),
                           inc="V" if k == KP - 1 else None,
                           waits=[ev_ab] if k == 0 else ())
                sch.edge("v", "V")
                for k in range(KP):
                    sch.op("v", lambda e, k=k: e.tensor_add(yk(k), yk(k), ln1[:]),
                           inc="V" if k == KP - 1 else None)
                sch.edge("v", "V")
                for k in range(KP):
                    sch.op("v", lambda e, k=k: e.tensor_scalar(
                        yk(k), yk(k), gb_sb[:, k:k + 1], gb_sb[:, 8 + k:9 + k],
                        OP.mult, OP.add),
                        inc="V" if k == KP - 1 else None,
                        waits=[ev_gb] if k == 0 else ())
                sch.edge("v", "V")
                ev_x2 = None
                for k in range(KP):
                    ev_x2 = sch.op("v", lambda e, k=k: e.tensor_add(
                        yk(k), yk(k), x_res[:, k * NT:(k + 1) * NT]),
                        inc="V" if k == KP - 1 else None,
                        waits=[ev_x1] if k == 0 else ())
                sch.edge("v", "V")
                free["x_res"] = ev_x2
                free["xn"] = ev_y_all
                ev_stx2 = sch.dma("g", featmaj(x2T)[:, :, c0:c0 + NT],
                                  acc[:].rearrange("p (k n) -> p k n", n=NT), "ST",
                                  waits=[ev_x2])
                if PHASES == "f1a":
                    sch.dma("g", featmaj(outT)[:, :, c0:c0 + NT],
                            acc[:].rearrange("p (k n) -> p k n", n=NT), "ST",
                            waits=[ev_x2])
                st_events[("x2", ti)] = ev_stx2
                prev_x2_st = ev_stx2
                ev_xn2, ev_pab2 = emit_ln(yk, ybf, ybf, ev_x2, "A", "B", sq2,
                                          cast_bf=True, apply=True,
                                          wait_dst_free=prev_xn2_st)
                pAB_rel["A"] = ev_pab2
                pAB_rel["B"] = ev_pab2
                ev_stxn2 = sch.dma(
                    "g", featmaj(xn2T)[:, :, c0:c0 + NT],
                    xs[:, 12288:12288 + KP * NT].rearrange("p (k n) -> p k n", n=NT),
                    "ST", waits=[ev_xn2])
                st_events[("xn2", ti)] = ev_stxn2
                prev_xn2_st = ev_stxn2
                free["acc"] = [ev_stx2, ("S", sch.cnt["S"]), ("T", sch.cnt["T"])]
                free_xs2 = [("T", sch.cnt["T"]), ev_stxn2]

            free["xs"] = free_xs2

        if PHASES == "f1":
            emit_ffn(1, xT, xbT, outT, "x1")
        else:
            emit_ffn(1, xT, xbT, x1T, "x1")
            emit_attn()
        bar2 = sch.barrier()
        wW_rel[0] = bar2
        wW_rel[1] = bar2
        wD_rel[0] = bar2
        wD_rel[1] = bar2
        pAB_rel["A"] = bar2
        pAB_rel["B"] = bar2
        for b in rel:
            rel[b] = bar2
        sig_rel[0] = bar2
        sig_rel[1] = bar2
        if PHASES == "all":
            emit_ffn(2, x2T, xn2T, outT, "out")

        sch.op("s", lambda e: e.activation(eps_t[:, 0:1], eps_t[:, 0:1], AF.Copy),
               waits=[("ST", sch.cnt["ST"])])
        sch.op("g", lambda e: e.engine_nop(), waits=[("AL", sch.cnt["AL"])])

        for qn in sch.lastw:
            for s, v in sch.lastw[qn].items():
                assert v <= sch.cnt[s], (qn, s, v, sch.cnt[s])

        with nc.Block() as block:
            block.tensor(lambda e: [f(e) for f in sch.q["t"]])
            block.vector(lambda e: [f(e) for f in sch.q["v"]])
            block.scalar(lambda e: [f(e) for f in sch.q["s"]])
            block.gpsimd(lambda e: [f(e) for f in sch.q["g"]])
            block.sync(lambda e: [f(e) for f in sch.q["y"]])

    return nc


def prep_inputs(L0_attn_x, L0_attn_Wq, L0_attn_bq, L0_attn_Wk, L0_attn_bk,
                L0_attn_Wv, L0_attn_bv, L0_attn_gamma, L0_attn_beta,
                L0_ffn_router_gate, L0_ffn_w_gate, L0_ffn_router_up, L0_ffn_w_up,
                L0_ffn_router_down, L0_ffn_w_down):
    f = np.float32
    x = np.asarray(L0_attn_x, f)
    B = x.shape[0]

    def blockify_gu(w):
        wt = np.asarray(w, f).transpose(0, 2, 1).reshape(E, KP, 128, MG, 128)
        return np.ascontiguousarray(wt.transpose(3, 2, 0, 1, 4))

    def blockify_d(w):
        wt = np.asarray(w, f).transpose(0, 2, 1).reshape(E, KD, 128, MD, 128)
        return np.ascontiguousarray(wt.transpose(3, 2, 0, 1, 4))

    gamma = np.asarray(L0_attn_gamma, f).reshape(KP, 128).T
    beta = np.asarray(L0_attn_beta, f).reshape(KP, 128).T
    shared = {
        "wgF": blockify_gu(L0_ffn_w_gate),
        "wuF": blockify_gu(L0_ffn_w_up),
        "wdF": blockify_d(L0_ffn_w_down),
        "wqF": np.ascontiguousarray(np.asarray(L0_attn_Wq, f).T),
        "wkF": np.ascontiguousarray(np.asarray(L0_attn_Wk, f).T),
        "wvF": np.ascontiguousarray(np.asarray(L0_attn_Wv, f).T),
        "rguF": np.ascontiguousarray(np.concatenate(
            [np.asarray(L0_ffn_router_gate, f).T,
             np.asarray(L0_ffn_router_up, f).T], axis=1)),
        "rdF": np.ascontiguousarray(np.asarray(L0_ffn_router_down, f).T),
        "browF": np.concatenate(
            [np.asarray(L0_attn_bq, f), np.asarray(L0_attn_bk, f),
             np.asarray(L0_attn_bv, f)]).reshape(1, 3 * D),
        "gbF": np.ascontiguousarray(np.concatenate([gamma, beta], axis=1)),
        "idF": np.eye(128, dtype=f),
    }
    in_maps = []
    for b in range(B):
        m = dict(shared)
        m["xT"] = np.ascontiguousarray(x[b].T)
        in_maps.append(m)
    return in_maps


def kernel(**inputs):
    in_maps = prep_inputs(**inputs)
    nc = build_program(n_tiles=T_FULL // NT)
    res = run_bass_kernel_spmd(nc, in_maps, core_ids=list(range(len(in_maps))))
    outs = [np.ascontiguousarray(np.asarray(r["outT"], np.float32).T)
            for r in res.results]
    return np.stack(outs, axis=0)
